# version 01
# baseline (speedup 1.0000x reference)
import math
import numpy as np
import ml_dtypes
from contextlib import ExitStack

import concourse.bass as bass
import concourse.mybir as mybir
from concourse.bass_utils import run_bass_kernel_spmd

F32 = mybir.dt.float32
BF16 = mybir.dt.bfloat16
I32 = mybir.dt.int32
I16 = mybir.dt.int16
AF = mybir.ActivationFunctionType
ALU = mybir.AluOpType

B, L, D = 4, 2048, 4096
HALF = 2048
TOK = 1024
NHEAD = 16
HD = 128
OSH = 512
SCALE = 1.0 / math.sqrt(HD)
NEG = -1.0e30


class Ctx:
    def __init__(self, nc, stack):
        self.nc = nc
        self.sems = {}
        for name in ("sd", "sc", "sv", "sm", "sp", "scc"):
            self.sems[name] = [stack.enter_context(nc.semaphore(name)), 0]

    def inc(self, name, instr, k):
        s = self.sems[name]
        instr.then_inc(s[0], k)
        s[1] += k

    def dma(self, eng, out, in_, sem="sd"):
        self.inc(sem, eng.dma_start(out, in_), 16)

    def wait(self, eng, *names):
        for n in names:
            s = self.sems[n]
            if s[1] > 0:
                eng.wait_ge(s[0], s[1])

    def barrier(self, *engs):
        for e in engs:
            self.wait(e, "sd", "sc", "sv", "sm", "sp", "scc")


def build_program(dbg=()):
    nc = bass.Bass("TRN2", target_bir_lowering=False)

    xT = nc.dram_tensor("xT", [D, L], F32, kind="ExternalInput").ap()
    x_rows = nc.dram_tensor("x_rows", [TOK, D], F32, kind="ExternalInput").ap()
    wqT = nc.dram_tensor("wqT", [D, HALF], F32, kind="ExternalInput").ap()
    wkT = nc.dram_tensor("wkT", [D, HALF], F32, kind="ExternalInput").ap()
    wvT = nc.dram_tensor("wvT", [D, HALF], F32, kind="ExternalInput").ap()
    woT = nc.dram_tensor("woT", [HALF, D], F32, kind="ExternalInput").ap()
    bq_t = nc.dram_tensor("bq_t", [128, NHEAD], F32, kind="ExternalInput").ap()
    bk_t = nc.dram_tensor("bk_t", [128, NHEAD], F32, kind="ExternalInput").ap()
    bv_r = nc.dram_tensor("bv_r", [1, HALF], F32, kind="ExternalInput").ap()
    bo_r = nc.dram_tensor("bo_r", [1, D], F32, kind="ExternalInput").ap()
    fb_r = nc.dram_tensor("fb_r", [1, D], F32, kind="ExternalInput").ap()
    qw = nc.dram_tensor("qw", [4, 128, 8 * 128], I32, kind="ExternalInput").ap()
    lut4_t = nc.dram_tensor("lut4_t", [128, 4 * 16], F32, kind="ExternalInput").ap()
    lut8_t = nc.dram_tensor("lut8_t", [128, 4 * 256], F32, kind="ExternalInput").ap()
    ident = nc.dram_tensor("ident", [128, 128], BF16, kind="ExternalInput").ap()
    ones_r = nc.dram_tensor("ones_r", [1, 128], BF16, kind="ExternalInput").ap()
    trineg = nc.dram_tensor("trineg", [128, 128], F32, kind="ExternalInput").ap()

    out_rows = nc.dram_tensor("out_rows", [TOK, D], F32, kind="ExternalOutput").ap()

    qT_sp = nc.dram_tensor("qT_sp", [HALF, L], BF16, kind="Internal").ap()
    v_sp = nc.dram_tensor("v_sp", [NHEAD, L, HD], BF16, kind="Internal").ap()
    x1_sp = nc.dram_tensor("x1_sp", [TOK, D], F32, kind="Internal").ap()
    z2_sp = nc.dram_tensor("z2_sp", [TOK, D], F32, kind="Internal").ap()
    rs_in = [nc.dram_tensor(f"rs_in{i}", [L, 1024], F32, kind="Internal").ap() for i in range(4)]
    rs_out = [nc.dram_tensor(f"rs_out{i}", [TOK, 1024], F32, kind="Internal").ap() for i in range(4)]
    ag_in = nc.dram_tensor("ag_in", [2 * D, OSH], BF16, kind="Internal").ap()
    ag_out = nc.dram_tensor("ag_out", [8 * 2 * D, OSH], BF16, kind="Internal").ap()

    dbg_t = {}
    if "qkv" in dbg:
        dbg_t["d_qT"] = nc.dram_tensor("d_qT", [HALF, L], BF16, kind="ExternalOutput").ap()
        dbg_t["d_kT"] = nc.dram_tensor("d_kT", [HALF, L], BF16, kind="ExternalOutput").ap()
        dbg_t["d_v"] = nc.dram_tensor("d_v", [NHEAD, L, HD], BF16, kind="ExternalOutput").ap()
    if "ao" in dbg:
        dbg_t["d_ao"] = nc.dram_tensor("d_ao", [HALF, L], BF16, kind="ExternalOutput").ap()
    if "x1" in dbg:
        dbg_t["d_x1"] = nc.dram_tensor("d_x1", [TOK, D], F32, kind="ExternalOutput").ap()
    if "w" in dbg:
        dbg_t["d_w"] = nc.dram_tensor("d_w", [2 * D, OSH], BF16, kind="ExternalOutput").ap()
    if "ago" in dbg:
        dbg_t["d_ago"] = nc.dram_tensor("d_ago", [8, 256, OSH], BF16, kind="ExternalOutput").ap()
    if "z2" in dbg:
        dbg_t["d_z2"] = nc.dram_tensor("d_z2", [TOK, D], F32, kind="ExternalOutput").ap()

    stack = ExitStack()
    with stack:
        cx = Ctx(nc, stack)

        stage = stack.enter_context(nc.sbuf_tensor("stage", [128, 4096], F32))
        ident_s = stack.enter_context(nc.sbuf_tensor("ident_s", [128, 128], BF16))
        ones_s = stack.enter_context(nc.sbuf_tensor("ones_s", [1, 128], BF16))
        trineg_s = stack.enter_context(nc.sbuf_tensor("trineg_s", [128, 128], F32))
        bias_bc = stack.enter_context(nc.sbuf_tensor("bias_bc", [128, D], F32))
        small = stack.enter_context(nc.sbuf_tensor("small", [128, 64], F32))
        rowbuf = stack.enter_context(nc.sbuf_tensor("rowbuf", [1, 512], F32))
        rowbf = stack.enter_context(nc.sbuf_tensor("rowbf", [1, 512], BF16))

        cx.dma(nc.sync, ident_s[:, :], ident[:, :])
        cx.dma(nc.sync, ones_s[:, :], ones_r[:, :])
        cx.dma(nc.sync, trineg_s[:, :], trineg[:, :])
        cx.dma(nc.sync, small[:, 0:NHEAD], bq_t[:, :])
        cx.dma(nc.sync, small[:, 16:16 + NHEAD], bk_t[:, :])
        cx.inc("sv", nc.vector.memset(small[:, 48:49], 1e-5), 1)

        def bias_broadcast(row_dram, dst_sbuf, width, psum):
            for j in range(width // 512):
                cx.wait(nc.sync, "sv", "sm")
                cx.dma(nc.sync, rowbuf[0:1, 0:512], row_dram[0:1, 512 * j:512 * j + 512])
                cx.wait(nc.vector, "sd", "sm")
                cx.inc("sv", nc.vector.tensor_copy(rowbf[0:1, 0:512], rowbuf[0:1, 0:512]), 1)
                cx.wait(nc.tensor, "sv")
                cx.inc("sm", nc.tensor.matmul(psum[:, 0:512], ones_s[:, :],
                                              rowbf[0:1, 0:512],
                                              start=True, stop=True), 1)
                cx.wait(nc.vector, "sm")
                cx.inc("sv", nc.vector.tensor_copy(dst_sbuf[:, 512 * j:512 * j + 512],
                                                   psum[:, 0:512]), 1)

        emit_dequant(cx, nc, qw, lut4_t, lut8_t, ag_in, ag_out, ident_s, dbg_t)
        with nc.sbuf_tensor("kT_res", [128, 16 * L], BF16) as kT_res:
            emit_proj(cx, nc, xT, wqT, wkT, wvT, bv_r, kT_res, qT_sp, v_sp,
                      stage, small, rowbuf, rowbf, ones_s, bias_broadcast, dbg_t)
            with nc.sbuf_tensor("aoT", [128, 16 * L], BF16) as aoT:
                emit_attn(cx, nc, kT_res, aoT, qT_sp, v_sp, ident_s, trineg_s, dbg_t)
                emit_wo_rs(cx, nc, aoT, woT, stage, rs_in, rs_out)
        emit_ln1_ffn_ln2(cx, nc, x_rows, x1_sp, z2_sp, rs_out, ag_out, bo_r, fb_r,
                         out_rows, stage, small, bias_bc, ident_s, bias_broadcast, dbg_t)

        cx.wait(nc.sync, "sp")
    return nc


def emit_dequant(cx, nc, qw, lut4_t, lut8_t, ag_in, ag_out, ident_s, dbg_t):
    with (
        nc.sbuf_tensor("qw_s", [128, 1024], I32) as qw_s,
        nc.sbuf_tensor("accI", [128, 4096], I32) as accI,
        nc.sbuf_tensor("tmpI", [128, 128], I32) as tmpI,
        nc.sbuf_tensor("idx16", [128, 4096], I16) as idx16,
        nc.sbuf_tensor("idx4", [128, 4096], I16) as idx4,
        nc.sbuf_tensor("lut4_s", [128, 64], F32) as lut4_s,
        nc.sbuf_tensor("lut8_s", [128, 1024], F32) as lut8_s,
        nc.sbuf_tensor("wtmp", [128, 4096], BF16) as wtmp,
        nc.sbuf_tensor("wacc", [128, 4096], BF16) as wacc,
        nc.sbuf_tensor("ttile", [128, 128], BF16) as ttile,
        nc.psum_tensor("pt_deq", [128, 128], BF16) as pt_deq,
    ):
        cx.dma(nc.sync, lut4_s[:, :], lut4_t[:, :])
        cx.dma(nc.sync, lut8_s[:, :], lut8_t[:, :])
        for ot in range(4):
            cx.wait(nc.sync, "sv")
            cx.dma(nc.sync, qw_s[:, :], qw[ot, :, :])
            cx.wait(nc.vector, "sd")
            acc3 = accI.ap().rearrange("p (w x) -> p w x", x=32)
            for s in range(32):
                cx.inc("sv", nc.vector.tensor_scalar(
                    acc3[:, :, s], qw_s[:, 0:128], int(s), 1,
                    ALU.logical_shift_right, ALU.bitwise_and), 1)
                for p in range(1, 8):
                    if s >= p:
                        cx.inc("sv", nc.vector.tensor_scalar(
                            tmpI[:, :], qw_s[:, 128 * p:128 * p + 128], int(s - p), int(1 << p),
                            ALU.logical_shift_right, ALU.bitwise_and), 1)
                    else:
                        cx.inc("sv", nc.vector.tensor_scalar(
                            tmpI[:, :], qw_s[:, 128 * p:128 * p + 128], int(p - s), int(1 << p),
                            ALU.logical_shift_left, ALU.bitwise_and), 1)
                    cx.inc("sv", nc.vector.tensor_tensor(
                        acc3[:, :, s], acc3[:, :, s], tmpI[:, :], ALU.bitwise_or), 1)
            cx.inc("sv", nc.vector.tensor_copy(idx16[:, :], accI[:, :]), 1)
            cx.inc("sv", nc.vector.tensor_single_scalar(idx4[:, :], idx16[:, :], 15,
                                                        ALU.bitwise_and), 1)
            for (nj, lut_s, base, dst0, idx_s) in ((16, lut4_s, 16 * ot, 0, idx4),
                                                   (256, lut8_s, 256 * ot, D, idx16)):
                cx.wait(nc.vector, "sm")
                for j in range(nj):
                    col = lut_s[:, base + j:base + j + 1]
                    tgt = wacc if j == 0 else wtmp
                    cx.inc("sv", nc.vector.tensor_scalar(
                        tgt[:, :], idx_s[:, :], int(j), col,
                        ALU.is_equal, ALU.mult), 1)
                    if j > 0:
                        cx.inc("sv", nc.vector.tensor_add(wacc[:, :], wacc[:, :], wtmp[:, :]), 1)
                cx.wait(nc.tensor, "sv")
                for k in range(32):
                    cx.wait(nc.tensor, "sv", "sp")
                    cx.inc("sm", nc.tensor.transpose(
                        pt_deq[:, :], wacc[:, 128 * k:128 * k + 128], ident_s[:, :]), 1)
                    cx.wait(nc.vector, "sm", "sp")
                    cx.inc("sv", nc.vector.tensor_copy(ttile[:, :], pt_deq[:, :]), 1)
                    cx.wait(nc.sync, "sv")
                    cx.dma(nc.sync, ag_in[dst0 + 128 * k: dst0 + 128 * k + 128,
                                          128 * ot:128 * ot + 128], ttile[:, :], sem="sp")
        cx.wait(nc.gpsimd, "sp")
        cx.inc("scc", nc.gpsimd.collective_compute(
            "AllGather", ALU.bypass,
            replica_groups=[[0, 1, 2, 3, 4, 5, 6, 7]],
            ins=[ag_in[:, :]], outs=[ag_out[:, :]]), 1)
        if "d_w" in dbg_t:
            cx.wait(nc.sync, "sp")
            cx.dma(nc.sync, dbg_t["d_w"][:, :], ag_in[:, :], sem="sp")


def emit_proj(cx, nc, xT, wqT, wkT, wvT, bv_r, kT_res, qT_sp, v_sp,
              stage, small, rowbuf, rowbf, ones_s, bias_broadcast, dbg_t):
    with (
        nc.sbuf_tensor("xt_h", [128, 32 * 1024], BF16) as xt_h,
        nc.sbuf_tensor("w_res", [128, 32 * 512], BF16) as w_res,
        nc.sbuf_tensor("vbias", [128, HALF], F32) as vbias,
        nc.sbuf_tensor("evq", [128, 512], BF16) as evq,
        nc.psum_tensor("pp0", [128, 512], F32) as pp0,
        nc.psum_tensor("pp1", [128, 512], F32) as pp1,
    ):
        bias_broadcast(bv_r[:, :], vbias, HALF, pp0)
        psums = [pp0, pp1]
        for th in range(2):
            t0 = 1024 * th
            for k in range(32):
                half = 1024 * (k % 2)
                cx.wait(nc.sync, "sv")
                cx.dma(nc.sync, stage[:, half:half + 1024],
                       xT[128 * k:128 * k + 128, t0:t0 + 1024])
                cx.wait(nc.vector, "sd")
                cx.inc("sv", nc.vector.tensor_copy(
                    xt_h[:, 1024 * k:1024 * k + 1024], stage[:, half:half + 1024]), 1)
            for (wsrc, kind) in ((wkT, "k"), (wvT, "v"), (wqT, "q")):
                for oq in range(4):
                    oq0 = 512 * oq
                    cx.wait(nc.sync, "sm")
                    for k in range(32):
                        half = 2048 + 512 * (k % 2)
                        cx.wait(nc.sync, "sc")
                        cx.dma(nc.sync, stage[:, half:half + 512],
                               wsrc[128 * k:128 * k + 128, oq0:oq0 + 512])
                        cx.wait(nc.scalar, "sd")
                        cx.inc("sc", nc.scalar.copy(
                            w_res[:, 512 * k:512 * k + 512], stage[:, half:half + 512]), 1)
                    cx.wait(nc.tensor, "sc", "sv")
                    if kind in ("q", "k"):
                        for otl in range(4):
                            og = 4 * oq + otl
                            for tb in range(2):
                                pb = psums[(otl * 2 + tb) % 2]
                                cx.wait(nc.tensor, "sc", "sp")
                                for k in range(32):
                                    cx.inc("sm", nc.tensor.matmul(
                                        pb[:, :],
                                        w_res[:, 512 * k + 128 * otl: 512 * k + 128 * otl + 128],
                                        xt_h[:, 1024 * k + 512 * tb: 1024 * k + 512 * tb + 512],
                                        start=(k == 0), stop=(k == 31)), 1)
                                cx.wait(nc.scalar, "sm")
                                bcol = (0 if kind == "q" else 16) + og
                                if kind == "k":
                                    cx.inc("sc", nc.scalar.add(
                                        kT_res[:, L * og + t0 + 512 * tb: L * og + t0 + 512 * tb + 512],
                                        pb[:, :], small[:, bcol:bcol + 1]), 1)
                                else:
                                    cx.wait(nc.scalar, "sp")
                                    cx.inc("sc", nc.scalar.add(evq[:, :], pb[:, :],
                                                               small[:, bcol:bcol + 1]), 1)
                                    cx.wait(nc.sync, "sc")
                                    cx.dma(nc.sync, qT_sp[128 * og:128 * og + 128,
                                                          t0 + 512 * tb: t0 + 512 * tb + 512],
                                           evq[:, :], sem="sp")
                    else:
                        for tt in range(8):
                            pb = psums[tt % 2]
                            cx.wait(nc.tensor, "sv", "sp")
                            for k in range(32):
                                cx.inc("sm", nc.tensor.matmul(
                                    pb[:, :],
                                    xt_h[:, 1024 * k + 128 * tt: 1024 * k + 128 * tt + 128],
                                    w_res[:, 512 * k:512 * k + 512],
                                    start=(k == 0), stop=(k == 31)), 1)
                            cx.wait(nc.vector, "sm", "sp")
                            cx.inc("sv", nc.vector.tensor_add(
                                evq[:, :], pb[:, :], vbias[:, oq0:oq0 + 512]), 1)
                            cx.wait(nc.sync, "sv")
                            for hh4 in range(4):
                                hgl = 4 * oq + hh4
                                cx.dma(nc.sync, v_sp[hgl, t0 + 128 * tt: t0 + 128 * tt + 128, :],
                                       evq[:, 128 * hh4:128 * hh4 + 128], sem="sp")
        if "d_qT" in dbg_t:
            cx.barrier(nc.sync)
            for og in range(16):
                cx.dma(nc.sync, dbg_t["d_kT"][128 * og:128 * og + 128, :],
                       kT_res[:, L * og:L * og + L], sem="sp")
            cx.dma(nc.sync, dbg_t["d_qT"][:, :], qT_sp[:, :], sem="sp")
            cx.dma(nc.sync, dbg_t["d_v"][:, :, :], v_sp[:, :, :], sem="sp")


def emit_attn(cx, nc, kT_res, aoT, qT_sp, v_sp, ident_s, trineg_s, dbg_t):
    with (
        nc.sbuf_tensor("vh", [128, 16 * 128], BF16) as vh,
        nc.sbuf_tensor("qh", [128, 512], BF16) as qh,
        nc.sbuf_tensor("ef", [128, 2048], F32) as ef,
        nc.sbuf_tensor("ebf", [128, 2048], BF16) as ebf,
        nc.sbuf_tensor("etb", [128, 16 * 512], BF16) as etb,
        nc.sbuf_tensor("zrow", [128, 16], F32) as zrow,
        nc.psum_tensor("ps_s", [128, 4 * 512], F32) as ps_s,
        nc.psum_tensor("ps_t", [128, 1024], BF16) as ps_t,
        nc.psum_tensor("ps_av", [128, 512], F32) as ps_av,
    ):
        vh3 = vh.ap().rearrange("p (k d) -> p k d", d=128)
        for h in range(NHEAD):
            cx.wait(nc.sync, "sm")
            cx.dma(nc.sync, vh3[:, :, :], v_sp[h, :, :].rearrange("(k p) d -> p k d", p=128))
            for sb in range(4):
                cx.wait(nc.sync, "sm")
                cx.dma(nc.sync, qh[:, :], qT_sp[128 * h:128 * h + 128, 512 * sb:512 * sb + 512])
                cx.wait(nc.tensor, "sd")
                for ti in range(4):
                    tq0 = 512 * sb + 128 * ti
                    ext = tq0 + 128
                    cx.wait(nc.tensor, "sc", "sv")
                    for tkb in range(sb + 1):
                        n = 512 if tkb < sb else 128 * (ti + 1)
                        cx.inc("sm", nc.tensor.matmul(
                            ps_s[:, 512 * tkb:512 * tkb + n],
                            qh[:, 128 * ti:128 * ti + 128],
                            kT_res[:, L * h + 512 * tkb: L * h + 512 * tkb + n],
                            start=True, stop=True), 1)
                    cx.wait(nc.vector, "sm")
                    cx.inc("sv", nc.vector.tensor_add(
                        ps_s[:, ext - 128:ext],
                        ps_s[:, ext - 128:ext], trineg_s[:, :]), 1)
                    cx.wait(nc.scalar, "sv")
                    nch = (ext + 511) // 512
                    for ch in range(nch):
                        e0 = 512 * ch
                        e1 = min(ext, e0 + 512)
                        cx.inc("sc", nc.scalar.activation(
                            ef[:, e0:e1], ps_s[:, e0:e1], AF.Exp, scale=SCALE,
                            accum_out=zrow[:, ch:ch + 1]), 1)
                    cx.wait(nc.vector, "sc", "sm")
                    cx.inc("sv", nc.vector.reduce_sum(zrow[:, 8:9], zrow[:, 0:nch],
                                                      axis=mybir.AxisListType.X), 1)
                    cx.wait(nc.scalar, "sv")
                    cx.inc("sc", nc.scalar.activation(zrow[:, 10:11], zrow[:, 8:9], AF.Ln), 1)
                    cx.wait(nc.vector, "sc")
                    cx.inc("sv", nc.vector.tensor_scalar(zrow[:, 11:12], zrow[:, 10:11],
                                                         -1.0, None, ALU.mult), 1)
                    sbext = 512 * sb + 512
                    cx.wait(nc.scalar, "sv")
                    for ch in range(nch):
                        e0 = 512 * ch
                        e1 = min(ext, e0 + 512)
                        cx.inc("sc", nc.scalar.activation(
                            ebf[:, e0:e1], ps_s[:, e0:e1], AF.Exp, scale=SCALE,
                            bias=zrow[:, 11:12]), 1)
                    if ext < sbext:
                        cx.inc("sv", nc.vector.memset(ebf[:, ext:sbext], 0.0), 1)
                    for k in range((ext + 127) // 128, 4 * (sb + 1)):
                        cx.inc("sv", nc.vector.memset(
                            etb[:, 512 * k + 128 * ti: 512 * k + 128 * ti + 128], 0.0), 1)
                    cx.wait(nc.tensor, "sv", "sc")
                    for k in range((ext + 127) // 128):
                        slot = k % 8
                        cx.wait(nc.tensor, "sv")
                        cx.inc("sm", nc.tensor.transpose(
                            ps_t[:, 128 * slot:128 * slot + 128],
                            ebf[:, 128 * k:128 * k + 128], ident_s[:, :]), 1)
                        cx.wait(nc.vector, "sm")
                        cx.inc("sv", nc.vector.tensor_copy(
                            etb[:, 512 * k + 128 * ti: 512 * k + 128 * ti + 128],
                            ps_t[:, 128 * slot:128 * slot + 128]), 1)
                cx.wait(nc.tensor, "sv")
                ntk_sb = 4 * (sb + 1)
                for k in range(ntk_sb):
                    cx.inc("sm", nc.tensor.matmul(
                        ps_av[:, :], vh[:, 128 * k:128 * k + 128],
                        etb[:, 512 * k:512 * k + 512],
                        start=(k == 0), stop=(k == ntk_sb - 1)), 1)
                cx.wait(nc.vector, "sm")
                cx.inc("sv", nc.vector.tensor_copy(
                    aoT[:, L * h + 512 * sb: L * h + 512 * sb + 512], ps_av[:, :]), 1)
        if "d_ao" in dbg_t:
            cx.barrier(nc.sync)
            for og in range(16):
                cx.dma(nc.sync, dbg_t["d_ao"][128 * og:128 * og + 128, :],
                       aoT[:, L * og:L * og + L], sem="sp")


def emit_wo_rs(cx, nc, aoT, woT, stage, rs_in, rs_out):
    with (
        nc.sbuf_tensor("womov", [128, 16 * 512], BF16) as womov,
        nc.sbuf_tensor("poev", [128, 512], F32) as poev,
        nc.psum_tensor("pw0", [128, 512], F32) as pw0,
        nc.psum_tensor("pw1", [128, 512], F32) as pw1,
    ):
        cx.wait(nc.vector, "sp")
        for ob in range(8):
            o0 = 512 * ob
            cx.wait(nc.sync, "sm")
            for k in range(16):
                half = 2048 + 512 * (k % 2)
                cx.wait(nc.sync, "sc")
                cx.dma(nc.sync, stage[:, half:half + 512],
                       woT[128 * k:128 * k + 128, o0:o0 + 512])
                cx.wait(nc.scalar, "sd")
                cx.inc("sc", nc.scalar.copy(
                    womov[:, 512 * k:512 * k + 512], stage[:, half:half + 512]), 1)
            cx.wait(nc.tensor, "sc")
            for tt in range(16):
                pb = pw0 if tt % 2 == 0 else pw1
                cx.wait(nc.tensor, "sv")
                for k in range(16):
                    cx.inc("sm", nc.tensor.matmul(
                        pb[:, :],
                        aoT[:, L * k + 128 * tt: L * k + 128 * tt + 128],
                        womov[:, 512 * k:512 * k + 512],
                        start=(k == 0), stop=(k == 15)), 1)
                cx.wait(nc.vector, "sm", "sp")
                cx.inc("sv", nc.vector.tensor_copy(poev[:, :], pb[:, :]), 1)
                cx.wait(nc.sync, "sv")
                cx.dma(nc.sync, rs_in[ob // 2][128 * tt:128 * tt + 128,
                                              512 * (ob % 2):512 * (ob % 2) + 512],
                       poev[:, :], sem="sp")
        cx.wait(nc.gpsimd, "sp")
        for c4 in range(4):
            cx.inc("scc", nc.gpsimd.collective_compute(
                "ReduceScatter", ALU.add,
                replica_groups=[[0, 1], [2, 3], [4, 5], [6, 7]],
                ins=[rs_in[c4][:, :]], outs=[rs_out[c4][:, :]]), 1)


def emit_ln1_ffn_ln2(cx, nc, x_rows, x1_sp, z2_sp, rs_out, ag_out, bo_r, fb_r,
                     out_rows, stage, small, bias_bc, ident_s, bias_broadcast, dbg_t):
    with nc.sbuf_tensor("x1T", [128, 32 * 1024], BF16) as x1T:
        with (
            nc.sbuf_tensor("zn", [128, 4096], F32) as zn,
            nc.sbuf_tensor("x1bf", [128, 4096], BF16) as x1bf,
            nc.sbuf_tensor("red", [128, 32], F32) as red,
            nc.psum_tensor("pb0", [128, 512], F32) as pb0,
            nc.psum_tensor("pt4", [128, 1024], BF16) as pt4,
        ):
            bias_broadcast(bo_r[:, :], bias_bc, D, pb0)
            cx.wait(nc.sync, "scc")
            for tt in range(8):
                cx.wait(nc.sync, "sv", "sc", "sm", "sp")
                for c4 in range(4):
                    cx.dma(nc.sync, zn[:, 1024 * c4:1024 * c4 + 1024],
                           rs_out[c4][128 * tt:128 * tt + 128, :])
                cx.dma(nc.sync, stage[:, 0:4096], x_rows[128 * tt:128 * tt + 128, :])
                cx.wait(nc.vector, "sd", "sc", "sm", "sp")
                cx.inc("sv", nc.vector.tensor_add(zn[:, :], zn[:, :], stage[:, 0:4096]), 1)
                cx.inc("sv", nc.vector.tensor_add(zn[:, :], zn[:, :], bias_bc[:, :]), 1)
                _ln_stats(cx, nc, zn, red, stage, small[:, 48:49])
                cx.wait(nc.vector, "sc")
                cx.inc("sv", nc.vector.tensor_scalar(zn[:, :], zn[:, :], red[:, 26:27], None, ALU.mult), 1)
                cx.wait(nc.sync, "sv")
                cx.dma(nc.sync, x1_sp[128 * tt:128 * tt + 128, :], zn[:, :], sem="sp")
                cx.wait(nc.scalar, "sv")
                for ch in range(8):
                    cx.inc("sc", nc.scalar.activation(stage[:, 512 * ch:512 * ch + 512],
                                                      zn[:, 512 * ch:512 * ch + 512], AF.Square,
                                                      accum_out=red[:, 16 + ch:17 + ch]), 1)
                cx.wait(nc.vector, "sc")
                cx.inc("sv", nc.vector.reduce_sum(red[:, 30:31], red[:, 16:24],
                                                  axis=mybir.AxisListType.X), 1)
                cx.wait(nc.scalar, "sv")
                cx.inc("sc", nc.scalar.mul(red[:, 31:32], red[:, 30:31], 1.0), 1)
                cx.wait(nc.vector, "sc")
                cx.inc("sv", nc.vector.tensor_scalar(small[:, 32 + tt:33 + tt], red[:, 31:32],
                                                     float(D), None, ALU.is_gt), 1)
                cx.inc("sv", nc.vector.tensor_copy(x1bf[:, :], zn[:, :]), 1)
                cx.wait(nc.tensor, "sv")
                for k in range(32):
                    slot = k % 8
                    cx.wait(nc.tensor, "sv")
                    cx.inc("sm", nc.tensor.transpose(pt4[:, 128 * slot:128 * slot + 128],
                                                     x1bf[:, 128 * k:128 * k + 128], ident_s[:, :]), 1)
                    cx.wait(nc.vector, "sm")
                    cx.inc("sv", nc.vector.tensor_copy(
                        x1T[:, 1024 * k + 128 * tt: 1024 * k + 128 * tt + 128],
                        pt4[:, 128 * slot:128 * slot + 128]), 1)
            if "d_x1" in dbg_t:
                cx.barrier(nc.sync)
                cx.dma(nc.sync, dbg_t["d_x1"][:, :], x1_sp[:, :], sem="sp")

        with (
            nc.sbuf_tensor("fmov", [128, 32 * 512], BF16) as fmov,
            nc.sbuf_tensor("y4b", [128, 4 * 512], F32) as y4b,
            nc.sbuf_tensor("ysel", [128, 512], F32) as ysel,
            nc.psum_tensor("pf0", [128, 512], F32) as pf0,
            nc.psum_tensor("pf1", [128, 512], F32) as pf1,
        ):
            fbias_bc = bias_bc
            bias_broadcast(fb_r[:, :], fbias_bc, D, pf0)
            cx.wait(nc.sync, "scc")
            for quad in range(2):
                for ob in range(8):
                    base = 8192 * ob
                    for mat in range(2):
                        cx.wait(nc.sync, "sm")
                        for k in range(32):
                            cx.dma(nc.sync, fmov[:, 512 * k:512 * k + 512],
                                   ag_out[base + 4096 * mat + 128 * k:
                                          base + 4096 * mat + 128 * k + 128, :])
                        cx.wait(nc.tensor, "sd")
                        for ti in range(4):
                            tt = 4 * quad + ti
                            pb = pf0 if ti % 2 == 0 else pf1
                            cx.wait(nc.tensor, "sv")
                            for k in range(32):
                                cx.inc("sm", nc.tensor.matmul(
                                    pb[:, :],
                                    x1T[:, 1024 * k + 128 * tt: 1024 * k + 128 * tt + 128],
                                    fmov[:, 512 * k:512 * k + 512],
                                    start=(k == 0), stop=(k == 31)), 1)
                            cx.wait(nc.vector, "sm")
                            if mat == 0:
                                cx.inc("sv", nc.vector.tensor_copy(
                                    y4b[:, 512 * ti:512 * ti + 512], pb[:, :]), 1)
                            else:
                                cx.inc("sv", nc.vector.tensor_sub(
                                    ysel[:, :], pb[:, :], y4b[:, 512 * ti:512 * ti + 512]), 1)
                                cx.inc("sv", nc.vector.scalar_tensor_tensor(
                                    ysel[:, :], ysel[:, :], small[:, 32 + tt:33 + tt],
                                    y4b[:, 512 * ti:512 * ti + 512], ALU.mult, ALU.add), 1)
                                cx.wait(nc.sync, "sv")
                                cx.dma(nc.sync, stage[:, 0:512],
                                       x1_sp[128 * tt:128 * tt + 128, 512 * ob:512 * ob + 512])
                                cx.wait(nc.vector, "sd")
                                cx.inc("sv", nc.vector.tensor_add(ysel[:, :], ysel[:, :],
                                                                  stage[:, 0:512]), 1)
                                cx.inc("sv", nc.vector.tensor_add(
                                    ysel[:, :], ysel[:, :],
                                    fbias_bc[:, 512 * ob:512 * ob + 512]), 1)
                                cx.wait(nc.sync, "sv")
                                cx.dma(nc.sync, z2_sp[128 * tt:128 * tt + 128,
                                                      512 * ob:512 * ob + 512], ysel[:, :], sem="sp")

        if "d_ago" in dbg_t:
            cx.wait(nc.sync, "scc", "sp")
            for r in range(8):
                cx.dma(nc.sync, dbg_t["d_ago"][r, 0:128, :],
                       ag_out[8192 * r:8192 * r + 128, :], sem="sp")
                cx.dma(nc.sync, dbg_t["d_ago"][r, 128:256, :],
                       ag_out[8192 * r + 4096:8192 * r + 4096 + 128, :], sem="sp")
        if "d_z2" in dbg_t:
            cx.wait(nc.sync, "sp")
            cx.dma(nc.sync, dbg_t["d_z2"][:, :], z2_sp[:, :], sem="sp")

        with (
            nc.sbuf_tensor("zn2", [128, 4096], F32) as zn2,
            nc.sbuf_tensor("red2", [128, 32], F32) as red2,
        ):
            cx.wait(nc.sync, "sp")
            for tt in range(8):
                cx.wait(nc.sync, "sv", "sc", "sp")
                cx.dma(nc.sync, zn2[:, :], z2_sp[128 * tt:128 * tt + 128, :])
                cx.wait(nc.vector, "sd", "sc")
                _ln_stats(cx, nc, zn2, red2, stage, small[:, 48:49])
                cx.wait(nc.vector, "sc")
                cx.inc("sv", nc.vector.tensor_scalar(zn2[:, :], zn2[:, :], red2[:, 26:27],
                                                     None, ALU.mult), 1)
                cx.wait(nc.sync, "sv")
                cx.dma(nc.sync, out_rows[128 * tt:128 * tt + 128, :], zn2[:, :], sem="sp")


def _ln_stats(cx, nc, zn, red, scratch, eps_ap):
    cx.inc("sv", nc.vector.reduce_sum(red[:, 0:1], zn[:, :],
                                      axis=mybir.AxisListType.X), 1)
    cx.wait(nc.scalar, "sv")
    cx.inc("sc", nc.scalar.mul(red[:, 9:10], red[:, 0:1], -1.0 / D), 1)
    cx.wait(nc.vector, "sc")
    cx.inc("sv", nc.vector.tensor_copy(red[:, 10:11], red[:, 9:10]), 1)
    cx.wait(nc.scalar, "sv")
    cx.inc("sc", nc.scalar.add(zn[:, :], zn[:, :], red[:, 10:11]), 1)
    for ch in range(8):
        cx.inc("sc", nc.scalar.activation(scratch[:, 512 * ch:512 * ch + 512],
                                          zn[:, 512 * ch:512 * ch + 512], AF.Square,
                                          accum_out=red[:, 16 + ch:17 + ch]), 1)
    cx.wait(nc.vector, "sc")
    cx.inc("sv", nc.vector.reduce_sum(red[:, 24:25], red[:, 16:24],
                                      axis=mybir.AxisListType.X), 1)
    cx.wait(nc.scalar, "sv")
    cx.inc("sc", nc.scalar.activation(red[:, 25:26], red[:, 24:25], AF.Ln,
                                      scale=1.0 / D, bias=eps_ap), 1)
    cx.wait(nc.vector, "sc")
    cx.inc("sv", nc.vector.tensor_scalar(red[:, 27:28], red[:, 25:26], -0.5, None, ALU.mult), 1)
    cx.wait(nc.scalar, "sv")
    cx.inc("sc", nc.scalar.activation(red[:, 26:27], red[:, 27:28], AF.Exp), 1)


def _layout_inputs(ins):
    x = np.ascontiguousarray(ins["L0_attn_x"])
    Wq, Wk, Wv, Wo = (np.asarray(ins["L0_attn_Wq"]), np.asarray(ins["L0_attn_Wk"]),
                      np.asarray(ins["L0_attn_Wv"]), np.asarray(ins["L0_attn_Wo"]))
    bq, bk, bv, bo = (np.asarray(ins["L0_attn_bq"]), np.asarray(ins["L0_attn_bk"]),
                      np.asarray(ins["L0_attn_bv"]), np.asarray(ins["L0_attn_bo"]))
    qweight = np.asarray(ins["L0_ffn_qweight"])
    lut4 = np.asarray(ins["L0_ffn_lut4"])
    lut8 = np.asarray(ins["L0_ffn_lut8"])
    fbias = np.asarray(ins["L0_ffn_bias"])

    ident = np.eye(128, dtype=ml_dtypes.bfloat16)
    ones = np.ones((1, 128), dtype=ml_dtypes.bfloat16)
    tri = np.where(np.arange(128)[None, :] > np.arange(128)[:, None],
                   np.float32(NEG), np.float32(0.0)).astype(np.float32)

    maps = []
    for c in range(8):
        b, hh = c // 2, c % 2
        dsl = slice(2048 * hh, 2048 * hh + 2048)
        osl = slice(512 * c, 512 * c + 512)
        qw_core = np.empty((4, 128, 8 * 128), dtype=np.int32)
        for ot in range(4):
            for p in range(8):
                qw_core[ot, :, 128 * p:128 * p + 128] = \
                    qweight[p, 512 * c + 128 * ot:512 * c + 128 * ot + 128, :]
        maps.append({
            "xT": np.ascontiguousarray(x[b].T),
            "x_rows": np.ascontiguousarray(x[b, 1024 * hh:1024 * hh + 1024, :]),
            "wqT": np.ascontiguousarray(Wq[dsl, :].T),
            "wkT": np.ascontiguousarray(Wk[dsl, :].T),
            "wvT": np.ascontiguousarray(Wv[dsl, :].T),
            "woT": np.ascontiguousarray(Wo.T[dsl, :]),
            "bq_t": np.ascontiguousarray(bq[dsl].reshape(16, 128).T),
            "bk_t": np.ascontiguousarray(bk[dsl].reshape(16, 128).T),
            "bv_r": np.ascontiguousarray(bv[dsl].reshape(1, 2048)),
            "bo_r": np.ascontiguousarray(bo.reshape(1, 4096)),
            "fb_r": np.ascontiguousarray(fbias.reshape(1, 4096)),
            "qw": qw_core,
            "lut4_t": np.ascontiguousarray(
                lut4[osl].reshape(4, 128, 16).transpose(1, 0, 2).reshape(128, 64)),
            "lut8_t": np.ascontiguousarray(
                lut8[osl].reshape(4, 128, 256).transpose(1, 0, 2).reshape(128, 1024)),
            "ident": ident,
            "ones_r": ones,
            "trineg": tri,
        })
    return maps


def kernel(**inputs):
    maps = _layout_inputs(inputs)
    nc = build_program()
    res = run_bass_kernel_spmd(nc, maps, core_ids=list(range(8)))
    out = np.empty((B, L, D), dtype=np.float32)
    for c in range(8):
        b, hh = c // 2, c % 2
        out[b, 1024 * hh:1024 * hh + 1024, :] = res.results[c]["out_rows"]
    return out


# version 02
# speedup vs baseline: 1.6754x; 1.6754x over previous
import math
import numpy as np
import ml_dtypes
from contextlib import ExitStack

import concourse.bass as bass
import concourse.mybir as mybir
from concourse.bass_utils import run_bass_kernel_spmd

F32 = mybir.dt.float32
BF16 = mybir.dt.bfloat16
I32 = mybir.dt.int32
I16 = mybir.dt.int16
AF = mybir.ActivationFunctionType
ALU = mybir.AluOpType

B, L, D = 4, 2048, 4096
HALF = 2048
TOK = 1024
NHEAD = 16
HD = 128
OSH = 512
SCALE = 1.0 / math.sqrt(HD)
NEG = -1.0e30
Z = ("sd", 0)


class Ctx:
    def __init__(self, nc, stack):
        self.nc = nc
        self.sems = {}
        for name in ("sd", "sc", "sv", "sm", "sp", "scc", "si", "sr",
                     "sxa", "sxb", "sw2", "sw3", "sw4", "sw5",
                     "se0", "se1", "svh0", "svh1", "skh0", "skh1",
                     "sqh0", "sqh1", "spo0", "spo1", "sfm0", "sfm1",
                     "sgl", "sgt", "sx1sp", "sz2", "srd", "sxr"):
            self.sems[name] = [stack.enter_context(nc.semaphore(name)), 0]

    def inc(self, name, instr, k):
        s = self.sems[name]
        instr.then_inc(s[0], k)
        s[1] += k
        return (name, s[1])

    def dma(self, eng, out, in_, sem):
        return self.inc(sem, eng.dma_start(out, in_), 16)

    def tok(self, name):
        return (name, self.sems[name][1])

    def wt(self, eng, *tokens):
        best = {}
        for t in tokens:
            if t is None:
                continue
            n, v = t
            if v > 0:
                best[n] = max(best.get(n, 0), v)
        for n, v in best.items():
            eng.wait_ge(self.sems[n][0], v)

    def wait(self, eng, *names):
        for n in names:
            s = self.sems[n]
            if s[1] > 0:
                eng.wait_ge(s[0], s[1])


def build_program(dbg=()):
    nc = bass.Bass("TRN2", target_bir_lowering=False)

    xT = nc.dram_tensor("xT", [D, L], F32, kind="ExternalInput").ap()
    x_rows = nc.dram_tensor("x_rows", [TOK, D], F32, kind="ExternalInput").ap()
    wqT = nc.dram_tensor("wqT", [D, HALF], F32, kind="ExternalInput").ap()
    wkT = nc.dram_tensor("wkT", [D, HALF], F32, kind="ExternalInput").ap()
    wvT = nc.dram_tensor("wvT", [D, HALF], F32, kind="ExternalInput").ap()
    woT = nc.dram_tensor("woT", [HALF, D], F32, kind="ExternalInput").ap()
    bq_t = nc.dram_tensor("bq_t", [128, NHEAD], F32, kind="ExternalInput").ap()
    bk_t = nc.dram_tensor("bk_t", [128, NHEAD], F32, kind="ExternalInput").ap()
    bv_r = nc.dram_tensor("bv_r", [1, HALF], F32, kind="ExternalInput").ap()
    bo_r = nc.dram_tensor("bo_r", [1, D], F32, kind="ExternalInput").ap()
    fb_r = nc.dram_tensor("fb_r", [1, D], F32, kind="ExternalInput").ap()
    qw32 = nc.dram_tensor("qw32", [4, 128, 8 * 128], I32, kind="ExternalInput").ap()
    lut4_t = nc.dram_tensor("lut4_t", [128, 4 * 16], F32, kind="ExternalInput").ap()
    lut8_t = nc.dram_tensor("lut8_t", [128, 4 * 256], F32, kind="ExternalInput").ap()
    ident = nc.dram_tensor("ident", [128, 128], BF16, kind="ExternalInput").ap()
    ones_r = nc.dram_tensor("ones_r", [1, 128], BF16, kind="ExternalInput").ap()
    trineg = nc.dram_tensor("trineg", [128, 128], F32, kind="ExternalInput").ap()

    out_rows = nc.dram_tensor("out_rows", [TOK, D], F32, kind="ExternalOutput").ap()

    qT_sp = nc.dram_tensor("qT_sp", [HALF, L], BF16, kind="Internal").ap()
    kT_sp = nc.dram_tensor("kT_sp", [HALF, L], BF16, kind="Internal").ap()
    v_sp = nc.dram_tensor("v_sp", [NHEAD, L, HD], BF16, kind="Internal").ap()
    x1_sp = nc.dram_tensor("x1_sp", [TOK, D], F32, kind="Internal").ap()
    z2_sp = nc.dram_tensor("z2_sp", [TOK, D], F32, kind="Internal").ap()
    rs_in = [nc.dram_tensor(f"rs_in{i}", [L, 1024], F32, kind="Internal").ap() for i in range(4)]
    rs_out = [nc.dram_tensor(f"rs_out{i}", [TOK, 1024], F32, kind="Internal").ap() for i in range(4)]
    ag_in = nc.dram_tensor("ag_in", [2 * D, OSH], BF16, kind="Internal").ap()
    ag_out = nc.dram_tensor("ag_out", [8 * 2 * D, OSH], BF16, kind="Internal").ap()

    dbg_t = {}
    if "x1" in dbg:
        dbg_t["d_x1"] = nc.dram_tensor("d_x1", [TOK, D], F32, kind="ExternalOutput").ap()
    if "z2" in dbg:
        dbg_t["d_z2"] = nc.dram_tensor("d_z2", [TOK, D], F32, kind="ExternalOutput").ap()
    if "w" in dbg:
        dbg_t["d_w"] = nc.dram_tensor("d_w", [2 * D, OSH], BF16, kind="ExternalOutput").ap()
    if "qkv" in dbg:
        dbg_t["d_qT"] = nc.dram_tensor("d_qT", [HALF, L], BF16, kind="ExternalOutput").ap()
        dbg_t["d_kT"] = nc.dram_tensor("d_kT", [HALF, L], BF16, kind="ExternalOutput").ap()
        dbg_t["d_v"] = nc.dram_tensor("d_v", [16, L, 128], BF16, kind="ExternalOutput").ap()
    if "ao" in dbg:
        dbg_t["d_ao"] = nc.dram_tensor("d_ao", [HALF, L], BF16, kind="ExternalOutput").ap()

    stack = ExitStack()
    with stack:
        cx = Ctx(nc, stack)
        stage = stack.enter_context(nc.sbuf_tensor("stage", [128, 4096], F32))
        ident_s = stack.enter_context(nc.sbuf_tensor("ident_s", [128, 128], BF16))
        ones_s = stack.enter_context(nc.sbuf_tensor("ones_s", [1, 128], BF16))
        trineg_s = stack.enter_context(nc.sbuf_tensor("trineg_s", [128, 128], F32))
        small = stack.enter_context(nc.sbuf_tensor("small", [128, 64], F32))
        rowbuf = stack.enter_context(nc.sbuf_tensor("rowbuf", [1, 512], F32))
        rowbf = stack.enter_context(nc.sbuf_tensor("rowbf", [1, 512], BF16))

        cx.dma(nc.sync, ident_s[:, :], ident[:, :], sem="si")
        cx.dma(nc.sync, ones_s[:, :], ones_r[:, :], sem="si")
        cx.dma(nc.sync, trineg_s[:, :], trineg[:, :], sem="si")
        cx.dma(nc.sync, small[:, 0:NHEAD], bq_t[:, :], sem="si")
        cx.dma(nc.sync, small[:, 16:16 + NHEAD], bk_t[:, :], sem="si")
        cx.inc("sv", nc.vector.memset(small[:, 48:49], 1e-5), 1)
        cx.wait(nc.tensor, "si")
        cx.wait(nc.scalar, "si")

        def bias_broadcast(row_dram, dst_sbuf, width, psum):
            for j in range(width // 512):
                cx.wait(nc.sync, "sv", "sm")
                cx.dma(nc.sync, rowbuf[0:1, 0:512], row_dram[0:1, 512 * j:512 * j + 512],
                       sem="sr")
                cx.wait(nc.vector, "sr", "sm", "si")
                cx.inc("sv", nc.vector.tensor_copy(rowbf[0:1, 0:512], rowbuf[0:1, 0:512]), 1)
                cx.wait(nc.tensor, "sv")
                cx.inc("sm", nc.tensor.matmul(psum[:, 0:512], ones_s[:, :],
                                              rowbf[0:1, 0:512], start=True, stop=True), 1)
                cx.wait(nc.vector, "sm")
                cx.inc("sv", nc.vector.tensor_copy(dst_sbuf[:, 512 * j:512 * j + 512],
                                                   psum[:, 0:512]), 1)

        with (
            nc.sbuf_tensor("qw_s", [128, 1024], I32) as qw_s,
            nc.sbuf_tensor("idx16", [128, 4096], I16) as idx16,
            nc.sbuf_tensor("idx4h", [128, 2048], I16) as idx4h,
            nc.sbuf_tensor("accI", [128, 4096], I32) as accI,
            nc.sbuf_tensor("tmpI", [128, 128], I32) as tmpI,
            nc.sbuf_tensor("lut4_s", [128, 64], F32) as lut4_s,
            nc.sbuf_tensor("lut8_s", [128, 256], F32) as lut8_s,
            nc.sbuf_tensor("wtmp", [128, 2048], BF16) as wtmp,
            nc.sbuf_tensor("wacc", [128, 2048], BF16) as wacc,
            nc.sbuf_tensor("ttile", [128, 512], BF16) as ttile,
            nc.psum_tensor("pt_deq", [128, 1024], BF16) as pt_deq,
        ):
            deq = _dequant_gen(cx, nc, qw32, lut4_t, lut8_t, ag_in, ag_out, ident_s,
                               qw_s, idx16, idx4h, accI, tmpI, lut4_s, lut8_s,
                               wtmp, wacc, ttile, pt_deq, dbg_t)

            def pump(n):
                for _ in range(n):
                    if next(deq, "DONE") == "DONE":
                        return

            emit_proj(cx, nc, xT, wqT, wkT, wvT, bv_r, kT_sp, qT_sp, v_sp,
                      stage, small, bias_broadcast, pump)
            if "d_qT" in dbg_t:
                cx.wait(nc.sync, "se0", "se1", "sv", "sc")
                cx.dma(nc.sync, dbg_t["d_qT"][:, :], qT_sp[:, :], sem="sp")
                cx.dma(nc.sync, dbg_t["d_kT"][:, :], kT_sp[:, :], sem="sp")
                cx.dma(nc.sync, dbg_t["d_v"][:, :, :], v_sp[:, :, :], sem="sp")
            with nc.sbuf_tensor("aoT", [128, 16 * L], BF16) as aoT:
                emit_attn(cx, nc, kT_sp, aoT, qT_sp, v_sp, ident_s, trineg_s, pump)
                if "d_ao" in dbg_t:
                    cx.wait(nc.sync, "sv", "sm")
                    for k in range(16):
                        cx.dma(nc.sync, dbg_t["d_ao"][128 * k:128 * k + 128, :],
                               aoT[:, L * k:L * k + L], sem="sp")
                    cx.wait(nc.sync, "sp")
                emit_wo_rs(cx, nc, aoT, woT, stage, rs_in, rs_out, pump)
            pump(10 ** 9)

        emit_ln1_ffn_ln2(cx, nc, x_rows, x1_sp, z2_sp, rs_out, ag_out, bo_r, fb_r,
                         out_rows, stage, small, ident_s, bias_broadcast, dbg_t)
        cx.wait(nc.sync, "sp")
    return nc


def _dequant_gen(cx, nc, qw32, lut4_t, lut8_t, ag_in, ag_out, ident_s,
                 qw_s, idx16, idx4h, accI, tmpI, lut4_s, lut8_s,
                 wtmp, wacc, ttile, pt_deq, dbg_t):
    cx.dma(nc.sync, lut4_s[:, :], lut4_t[:, :], sem="si")
    jl_tok = Z
    tp_tok = Z
    ttile_dma = Z

    ev_prev = Z

    def jloop(nj, lut_s, base, dst0, idx_ap, i0, ot):
        nonlocal jl_tok, tp_tok, ttile_dma, ev_prev
        cx.wt(nc.vector, tp_tok)
        for j in range(nj):
            col = lut_s[:, base + j:base + j + 1]
            tgt = wacc if j == 0 else wtmp
            jl_tok = cx.inc("sv", nc.vector.tensor_scalar(
                tgt[:, :], idx_ap, int(j), col,
                ALU.is_equal, ALU.mult), 1)
            if j > 0:
                jl_tok = cx.inc("sv", nc.vector.tensor_add(
                    wacc[:, :], wacc[:, :], wtmp[:, :]), 1)
            if j % 16 == 15:
                yield
        for g in range(4):
            cx.wt(nc.tensor, jl_tok, ttile_dma, ev_prev)
            for k in range(4):
                kk = 4 * g + k
                tp_tok = cx.inc("sm", nc.tensor.transpose(
                    pt_deq[:, 128 * k:128 * k + 128],
                    wacc[:, 128 * kk:128 * kk + 128], ident_s[:, :]), 1)
            cx.wt(nc.vector, tp_tok, ttile_dma)
            ev = cx.inc("sv", nc.vector.tensor_copy(ttile[:, :], pt_deq[:, 0:512]), 1)
            ev_prev = ev
            cx.wt(nc.gpsimd, ev)
            for k in range(4):
                kk = 4 * g + k
                ttile_dma = cx.dma(
                    nc.gpsimd,
                    ag_in[dst0 + i0 + 128 * kk: dst0 + i0 + 128 * kk + 128,
                          128 * ot:128 * ot + 128],
                    ttile[:, 128 * k:128 * k + 128], sem="sgt")
            if g % 2 == 1:
                yield

    for ot in range(4):
        cx.wt(nc.gpsimd, jl_tok)
        qw_tok = cx.dma(nc.gpsimd, qw_s[:, :], qw32[ot, :, :], sem="sgl")
        cx.wt(nc.gpsimd, jl_tok)
        l8tok = cx.dma(nc.gpsimd, lut8_s[:, :], lut8_t[:, 256 * ot:256 * ot + 256], sem="sgl")
        cx.wt(nc.vector, qw_tok, l8tok, cx.tok("si") if ot == 0 else None)
        acc3 = accI.ap().rearrange("p (w x) -> p w x", x=32)
        for s in range(32):
            for p in range(8):
                if p == 0:
                    jl_tok = cx.inc("sv", nc.vector.tensor_scalar(
                        acc3[:, :, s], qw_s[:, 0:128], int(s), 1,
                        ALU.logical_shift_right, ALU.bitwise_and), 1)
                else:
                    srcp = qw_s[:, 128 * p:128 * p + 128]
                    if s >= p:
                        jl_tok = cx.inc("sv", nc.vector.tensor_scalar(
                            tmpI[:, :], srcp, int(s - p), int(1 << p),
                            ALU.logical_shift_right, ALU.bitwise_and), 1)
                    else:
                        jl_tok = cx.inc("sv", nc.vector.tensor_scalar(
                            tmpI[:, :], srcp, int(p - s), int(1 << p),
                            ALU.logical_shift_left, ALU.bitwise_and), 1)
                    jl_tok = cx.inc("sv", nc.vector.tensor_tensor(
                        acc3[:, :, s], acc3[:, :, s], tmpI[:, :], ALU.bitwise_or), 1)
            if s % 8 == 7:
                yield
        jl_tok = cx.inc("sv", nc.vector.tensor_copy(idx16[:, :], accI[:, :]), 1)
        yield
        for ih in range(2):
            i0 = 2048 * ih
            jl_tok = cx.inc("sv", nc.vector.tensor_single_scalar(
                idx4h[:, :], idx16[:, i0:i0 + 2048], 15, ALU.bitwise_and), 1)
            yield from jloop(16, lut4_s, 16 * ot, 0, idx4h[:, :], i0, ot)
            yield from jloop(256, lut8_s, 0, D, idx16[:, i0:i0 + 2048], i0, ot)
    cx.wt(nc.gpsimd, ttile_dma)
    cx.inc("scc", nc.gpsimd.collective_compute(
        "AllGather", ALU.bypass,
        replica_groups=[[0, 1, 2, 3, 4, 5, 6, 7]],
        ins=[ag_in[:, :]], outs=[ag_out[:, :]]), 1)
    if "d_w" in dbg_t:
        cx.wait(nc.sync, "sgt")
        cx.dma(nc.sync, dbg_t["d_w"][:, :], ag_in[:, :], sem="sp")


def emit_proj(cx, nc, xT, wqT, wkT, wvT, bv_r, kT_sp, qT_sp, v_sp,
              stage, small, bias_broadcast, pump):
    with (
        nc.sbuf_tensor("xt_h", [128, 32 * 1024], BF16) as xt_h,
        nc.sbuf_tensor("wbuf0", [128, 32 * 128], BF16) as wbuf0,
        nc.sbuf_tensor("wbuf1", [128, 32 * 128], BF16) as wbuf1,
        nc.sbuf_tensor("vbias", [128, HALF], BF16) as vbias,
        nc.sbuf_tensor("evq0", [128, 512], BF16) as evq0,
        nc.sbuf_tensor("evq1", [128, 512], BF16) as evq1,
        nc.psum_tensor("pp0", [128, 512], F32) as pp0,
        nc.psum_tensor("pp1", [128, 512], F32) as pp1,
    ):
        bias_broadcast(bv_r[:, :], vbias, HALF, pp0)
        vbias_tok = cx.tok("sv")
        wbufs = [wbuf0, wbuf1]
        wfree = [Z, Z]
        wready = [Z, Z]
        psums = [pp0, pp1]
        pfree = [Z, Z]
        evqs = [evq0, evq1]
        efree = [Z, Z]
        stg = [Z] * 6

        def load_w_16th(wsrc, os_, buf):
            o0 = 128 * os_
            ctok = Z
            for k in range(32):
                sl = 2 + k % 4
                cx.wt(nc.sync, stg[sl], wfree[buf] if k == 0 else None)
                dtok = cx.dma(nc.sync, stage[:, 1024 + 256 * (k % 4):1024 + 256 * (k % 4) + 128],
                              wsrc[128 * k:128 * k + 128, o0:o0 + 128], sem=f"sw{sl}")
                cx.wt(nc.scalar, dtok, wfree[buf] if k == 0 else None)
                ctok = cx.inc("sc", nc.scalar.copy(
                    wbufs[buf][:, 128 * k:128 * k + 128],
                    stage[:, 1024 + 256 * (k % 4):1024 + 256 * (k % 4) + 128]), 1)
                stg[sl] = ctok
            wready[buf] = ctok

        for th in range(2):
            t0 = 1024 * th
            xt_last = Z
            for k in range(32):
                sl = k % 2
                xsem = "sxa" if sl == 0 else "sxb"
                cx.wt(nc.sync, stg[sl])
                cx.dma(nc.sync, stage[:, 512 * sl:512 * sl + 512],
                       xT[128 * k:128 * k + 128, t0:t0 + 512], sem=xsem)
                dtok2 = cx.dma(nc.sync, stage[:, 512 * sl + 2048:512 * sl + 2048 + 512],
                               xT[128 * k:128 * k + 128, t0 + 512:t0 + 1024], sem=xsem)
                cx.wt(nc.vector, dtok2, wfree[0] if k == 0 else None,
                      wfree[1] if k == 0 else None)
                cx.inc("sv", nc.vector.tensor_copy(
                    xt_h[:, 1024 * k:1024 * k + 512], stage[:, 512 * sl:512 * sl + 512]), 1)
                ctok = cx.inc("sv", nc.vector.tensor_copy(
                    xt_h[:, 1024 * k + 512:1024 * k + 1024],
                    stage[:, 512 * sl + 2048:512 * sl + 2048 + 512]), 1)
                stg[sl] = ctok
                xt_last = ctok
            pump(3)
            for (wsrc, kind) in ((wkT, "k"), (wvT, "v"), (wqT, "q")):
                load_w_16th(wsrc, 0, 0)
                for os_ in range(16):
                    buf = os_ % 2
                    if os_ < 15:
                        load_w_16th(wsrc, os_ + 1, (os_ + 1) % 2)
                    g = 0
                    og = os_
                    if kind in ("q", "k"):
                        for tb in range(2):
                            pb = psums[g % 2]
                            cx.wt(nc.tensor, wready[buf], xt_last, pfree[g % 2])
                            mt = Z
                            for k in range(32):
                                mt = cx.inc("sm", nc.tensor.matmul(
                                    pb[:, :],
                                    wbufs[buf][:, 128 * k:128 * k + 128],
                                    xt_h[:, 1024 * k + 512 * tb: 1024 * k + 512 * tb + 512],
                                    start=(k == 0), stop=(k == 31)), 1)
                            wfree[buf] = mt
                            bcol = (0 if kind == "q" else 16) + og
                            spt = qT_sp if kind == "q" else kT_sp
                            cx.wt(nc.scalar, mt, efree[g % 2])
                            ev = cx.inc("sc", nc.scalar.add(
                                evqs[g % 2][:, :], pb[:, :], small[:, bcol:bcol + 1]), 1)
                            cx.wt(nc.sync, ev)
                            efree[g % 2] = cx.dma(
                                nc.sync, spt[128 * og:128 * og + 128,
                                             t0 + 512 * tb: t0 + 512 * tb + 512],
                                evqs[g % 2][:, :], sem=f"se{g % 2}")
                            pfree[g % 2] = ev
                            g += 1
                        if os_ % 2 == 1:
                            pump(1)
                    else:
                        for tt in range(8):
                            pb = psums[g % 2]
                            cx.wt(nc.tensor, wready[buf], xt_last, pfree[g % 2])
                            mt = Z
                            for k in range(32):
                                mt = cx.inc("sm", nc.tensor.matmul(
                                    pb[:, 0:128],
                                    xt_h[:, 1024 * k + 128 * tt: 1024 * k + 128 * tt + 128],
                                    wbufs[buf][:, 128 * k:128 * k + 128],
                                    start=(k == 0), stop=(k == 31)), 1)
                            wfree[buf] = mt
                            cx.wt(nc.vector, mt, vbias_tok, efree[g % 2])
                            ev = cx.inc("sv", nc.vector.tensor_add(
                                evqs[g % 2][:, 0:128], pb[:, 0:128],
                                vbias[:, 128 * os_:128 * os_ + 128]), 1)
                            cx.wt(nc.sync, ev)
                            efree[g % 2] = cx.dma(
                                nc.sync, v_sp[os_, t0 + 128 * tt: t0 + 128 * tt + 128, :],
                                evqs[g % 2][:, 0:128], sem=f"se{g % 2}")
                            pfree[g % 2] = ev
                            g += 1
                        if tt == 7 and os_ % 2 == 1:
                            pump(1)


def emit_attn(cx, nc, kT_sp, aoT, qT_sp, v_sp, ident_s, trineg_s, pump):
    kq_tok = cx.tok("sc")
    sp_q0, sp_q1 = cx.tok("se0"), cx.tok("se1")
    with (
        nc.sbuf_tensor("kh0", [128, L], BF16) as kh0,
        nc.sbuf_tensor("kh1", [128, L], BF16) as kh1,
        nc.sbuf_tensor("vh0", [128, 16 * 128], BF16) as vh0,
        nc.sbuf_tensor("vh1", [128, 16 * 128], BF16) as vh1,
        nc.sbuf_tensor("qh0", [128, 512], BF16) as qh0,
        nc.sbuf_tensor("qh1", [128, 512], BF16) as qh1,
        nc.sbuf_tensor("ef", [128, 512], F32) as ef,
        nc.sbuf_tensor("ebf", [128, 2048], BF16) as ebf,
        nc.sbuf_tensor("etb", [128, 16 * 512], BF16) as etb,
        nc.sbuf_tensor("zrow", [128, 16], F32) as zrow,
        nc.psum_tensor("ps_s", [128, 4 * 512], F32) as ps_s,
        nc.psum_tensor("ps_t", [128, 2048], BF16) as ps_t,
        nc.psum_tensor("ps_av", [128, 512], F32) as ps_av,
    ):
        vhs, qhs, khs = [vh0, vh1], [qh0, qh1], [kh0, kh1]
        vh_tok = [Z, Z]
        vh_free = [Z, Z]
        kh_tok = [Z, Z]
        kh_free = [Z, Z]
        qh_tok = [Z, Z]
        qh_free = [Z, Z]
        exp_tok = Z
        ebf_free = Z
        et_ev = Z
        et_ms = Z
        av_tok = Z
        ao_ev = Z
        ptg_free = [Z, Z]
        vh3 = [v.ap().rearrange("p (k d) -> p k d", d=128) for v in vhs]
        cx.wt(nc.sync, kq_tok, sp_q0, sp_q1)
        vh_tok[0] = cx.dma(nc.sync, vh3[0][:, :, :],
                           v_sp[0, :, :].rearrange("(k p) d -> p k d", p=128), sem="svh0")
        kh_tok[0] = cx.dma(nc.sync, khs[0][:, :], kT_sp[0:128, :], sem="skh0")
        for h in range(NHEAD):
            hb = h % 2
            if h + 1 < NHEAD:
                cx.wt(nc.sync, vh_free[(h + 1) % 2])
                vh_tok[(h + 1) % 2] = cx.dma(nc.sync, vh3[(h + 1) % 2][:, :, :],
                                             v_sp[h + 1, :, :].rearrange("(k p) d -> p k d", p=128),
                                             sem=f"svh{(h + 1) % 2}")
                cx.wt(nc.sync, kh_free[(h + 1) % 2])
                kh_tok[(h + 1) % 2] = cx.dma(nc.sync, khs[(h + 1) % 2][:, :],
                                             kT_sp[128 * (h + 1):128 * (h + 1) + 128, :],
                                             sem=f"skh{(h + 1) % 2}")
            for sb in range(4):
                qb = sb % 2
                cx.wt(nc.sync, qh_free[qb])
                qh_tok[qb] = cx.dma(nc.sync, qhs[qb][:, :],
                                    qT_sp[128 * h:128 * h + 128, 512 * sb:512 * sb + 512],
                                    sem=f"sqh{qb}")
                for ti in range(4):
                    tq0 = 512 * sb + 128 * ti
                    ext = tq0 + 128
                    cx.wt(nc.tensor, exp_tok, qh_tok[qb], kh_tok[hb])
                    mm_tok = Z
                    for tkb in range(sb + 1):
                        n = 512 if tkb < sb else 128 * (ti + 1)
                        mm_tok = cx.inc("sm", nc.tensor.matmul(
                            ps_s[:, 512 * tkb:512 * tkb + n],
                            qhs[qb][:, 128 * ti:128 * ti + 128],
                            khs[hb][:, 512 * tkb: 512 * tkb + n],
                            start=True, stop=True), 1)
                    qh_free[qb] = mm_tok
                    kh_free[hb] = mm_tok
                    cx.wt(nc.vector, mm_tok)
                    msk = cx.inc("sv", nc.vector.tensor_add(
                        ps_s[:, ext - 128:ext], ps_s[:, ext - 128:ext], trineg_s[:, :]), 1)
                    cx.wt(nc.scalar, msk)
                    nch = (ext + 511) // 512
                    ac = Z
                    for ch in range(nch):
                        e0 = 512 * ch
                        e1 = min(ext, e0 + 512)
                        ac = cx.inc("sc", nc.scalar.activation(
                            ef[:, 0:e1 - e0], ps_s[:, e0:e1], AF.Exp, scale=SCALE,
                            accum_out=zrow[:, ch:ch + 1]), 1)
                    cx.wt(nc.vector, ac)
                    red = cx.inc("sv", nc.vector.reduce_sum(zrow[:, 8:9], zrow[:, 0:nch],
                                                            axis=mybir.AxisListType.X), 1)
                    cx.wt(nc.scalar, red)
                    lnt = cx.inc("sc", nc.scalar.activation(zrow[:, 10:11], zrow[:, 8:9], AF.Ln), 1)
                    cx.wt(nc.vector, lnt)
                    neg = cx.inc("sv", nc.vector.tensor_scalar(zrow[:, 11:12], zrow[:, 10:11],
                                                               -1.0, None, ALU.mult), 1)
                    cx.wt(nc.scalar, neg, ebf_free)
                    for ch in range(nch):
                        e0 = 512 * ch
                        e1 = min(ext, e0 + 512)
                        exp_tok = cx.inc("sc", nc.scalar.activation(
                            ebf[:, e0:e1], ps_s[:, e0:e1], AF.Exp, scale=SCALE,
                            bias=zrow[:, 11:12]), 1)
                    ntk = (ext + 127) // 128
                    cx.wt(nc.vector, av_tok)
                    for k in range(ntk, 4 * (sb + 1)):
                        et_ms = cx.inc("sv", nc.vector.memset(
                            etb[:, 512 * k + 128 * ti: 512 * k + 128 * ti + 128], 0.0), 1)
                    for g in range((ntk + 3) // 4):
                        gb = g % 2
                        k0 = 4 * g
                        kn = min(4, ntk - k0)
                        cx.wt(nc.tensor, exp_tok, ptg_free[gb])
                        tt_ = Z
                        for k in range(kn):
                            tt_ = cx.inc("sm", nc.tensor.transpose(
                                ps_t[:, 1024 * gb + 128 * k:1024 * gb + 128 * k + 128],
                                ebf[:, 128 * (k0 + k):128 * (k0 + k) + 128], ident_s[:, :]), 1)
                        ebf_free = tt_
                        cx.wt(nc.vector, tt_, av_tok)
                        dst = etb.ap().rearrange("p (k t) -> p k t", t=512)
                        ev = cx.inc("sv", nc.vector.tensor_copy(
                            dst[:, k0:k0 + kn, 128 * ti:128 * ti + 128],
                            ps_t.ap().rearrange("p (k t) -> p k t", t=128)[:, 8 * gb:8 * gb + kn, :]), 1)
                        ptg_free[gb] = ev
                        et_ev = ev
                cx.wt(nc.tensor, et_ev, et_ms, vh_tok[hb], ao_ev)
                ntk_sb = 4 * (sb + 1)
                for k in range(ntk_sb):
                    av_tok = cx.inc("sm", nc.tensor.matmul(
                        ps_av[:, :], vhs[hb][:, 128 * k:128 * k + 128],
                        etb[:, 512 * k:512 * k + 512],
                        start=(k == 0), stop=(k == ntk_sb - 1)), 1)
                vh_free[hb] = av_tok
                cx.wt(nc.vector, av_tok)
                ao_ev = cx.inc("sv", nc.vector.tensor_copy(
                    aoT[:, L * h + 512 * sb: L * h + 512 * sb + 512], ps_av[:, :]), 1)
                pump(1)
            pump(3)


def emit_wo_rs(cx, nc, aoT, woT, stage, rs_in, rs_out, pump):
    ao_done = cx.tok("sv")
    with (
        nc.sbuf_tensor("wo0", [128, 16 * 512], BF16) as wo0,
        nc.sbuf_tensor("wo1", [128, 16 * 512], BF16) as wo1,
        nc.sbuf_tensor("poev0", [128, 512], F32) as poev0,
        nc.sbuf_tensor("poev1", [128, 512], F32) as poev1,
        nc.psum_tensor("pw0", [128, 512], F32) as pw0,
        nc.psum_tensor("pw1", [128, 512], F32) as pw1,
    ):
        wos = [wo0, wo1]
        wready = [Z, Z]
        wfree = [Z, Z]
        poevs = [poev0, poev1]
        pofree = [Z, Z]
        pfree = [Z, Z]
        psums = [pw0, pw1]
        stg = [cx.tok("sc")] * 4

        def load_wo(ob, buf):
            ctok = Z
            o0 = 512 * ob
            for k in range(16):
                sl = k % 4
                cx.wt(nc.sync, stg[sl], wfree[buf] if k == 0 else None)
                dtok = cx.dma(nc.sync, stage[:, 1024 * sl:1024 * sl + 512],
                              woT[128 * k:128 * k + 128, o0:o0 + 512], sem=f"sw{2 + sl}")
                cx.wt(nc.scalar, dtok, wfree[buf] if k == 0 else None)
                ctok = cx.inc("sc", nc.scalar.copy(
                    wos[buf][:, 512 * k:512 * k + 512], stage[:, 1024 * sl:1024 * sl + 512]), 1)
                stg[sl] = ctok
            wready[buf] = ctok

        load_wo(0, 0)
        for ob in range(8):
            buf = ob % 2
            if ob < 7:
                load_wo(ob + 1, (ob + 1) % 2)
            for tt in range(16):
                g = tt % 2
                pb = psums[g]
                cx.wt(nc.tensor, wready[buf], ao_done, pfree[g])
                mt = Z
                for k in range(16):
                    mt = cx.inc("sm", nc.tensor.matmul(
                        pb[:, :],
                        aoT[:, L * k + 128 * tt: L * k + 128 * tt + 128],
                        wos[buf][:, 512 * k:512 * k + 512],
                        start=(k == 0), stop=(k == 15)), 1)
                wfree[buf] = mt
                cx.wt(nc.vector, mt, pofree[g])
                ev = cx.inc("sv", nc.vector.tensor_copy(poevs[g][:, :], pb[:, :]), 1)
                pfree[g] = ev
                cx.wt(nc.sync, ev)
                pofree[g] = cx.dma(nc.sync, rs_in[ob // 2][128 * tt:128 * tt + 128,
                                                          512 * (ob % 2):512 * (ob % 2) + 512],
                                   poevs[g][:, :], sem=f"spo{g}")
            pump(2)
            if ob % 2 == 1:
                c4 = ob // 2
                cx.wt(nc.gpsimd, pofree[0], pofree[1])
                cx.inc("scc", nc.gpsimd.collective_compute(
                    "ReduceScatter", ALU.add,
                    replica_groups=[[0, 1], [2, 3], [4, 5], [6, 7]],
                    ins=[rs_in[c4][:, :]], outs=[rs_out[c4][:, :]]), 1)


def emit_ln1_ffn_ln2(cx, nc, x_rows, x1_sp, z2_sp, rs_out, ag_out, bo_r, fb_r,
                     out_rows, stage, small, ident_s, bias_broadcast, dbg_t):
    with (
        nc.sbuf_tensor("x1T", [128, 32 * 1024], BF16) as x1T,
        nc.sbuf_tensor("bias_bc", [128, D], F32) as bias_bc,
    ):
        with (
            nc.sbuf_tensor("zn", [128, 4096], F32) as zn,
            nc.sbuf_tensor("x1bf", [128, 4096], BF16) as x1bf,
            nc.sbuf_tensor("red", [128, 32], F32) as red,
            nc.psum_tensor("pb0", [128, 512], F32) as pb0,
            nc.psum_tensor("pt4", [128, 2048], BF16) as pt4,
        ):
            bias_broadcast(bo_r[:, :], bias_bc, D, pb0)
            cx.wait(nc.sync, "scc")
            ptg_free = [Z, Z]
            for tt in range(8):
                cx.wait(nc.sync, "sv", "sc", "sm", "sx1sp")
                for c4 in range(4):
                    cx.dma(nc.sync, zn[:, 1024 * c4:1024 * c4 + 1024],
                           rs_out[c4][128 * tt:128 * tt + 128, :], sem="srd")
                cx.dma(nc.sync, stage[:, 0:4096], x_rows[128 * tt:128 * tt + 128, :], sem="srd")
                cx.wait(nc.vector, "srd", "sc", "sm", "sx1sp")
                cx.inc("sv", nc.vector.tensor_add(zn[:, :], zn[:, :], stage[:, 0:4096]), 1)
                cx.inc("sv", nc.vector.tensor_add(zn[:, :], zn[:, :], bias_bc[:, :]), 1)
                _ln_stats(cx, nc, zn, red, stage, small[:, 48:49])
                cx.wait(nc.vector, "sc")
                cx.inc("sv", nc.vector.tensor_scalar(zn[:, :], zn[:, :], red[:, 26:27], None, ALU.mult), 1)
                cx.wait(nc.sync, "sv")
                cx.dma(nc.sync, x1_sp[128 * tt:128 * tt + 128, :], zn[:, :], sem="sx1sp")
                cx.wait(nc.scalar, "sv")
                for ch in range(8):
                    cx.inc("sc", nc.scalar.activation(stage[:, 512 * ch:512 * ch + 512],
                                                      zn[:, 512 * ch:512 * ch + 512], AF.Square,
                                                      accum_out=red[:, 16 + ch:17 + ch]), 1)
                cx.wait(nc.vector, "sc")
                cx.inc("sv", nc.vector.reduce_sum(red[:, 30:31], red[:, 16:24],
                                                  axis=mybir.AxisListType.X), 1)
                cx.wait(nc.scalar, "sv")
                cx.inc("sc", nc.scalar.mul(red[:, 31:32], red[:, 30:31], 1.0), 1)
                cx.wait(nc.vector, "sc")
                cx.inc("sv", nc.vector.tensor_scalar(small[:, 32 + tt:33 + tt], red[:, 31:32],
                                                     float(D), None, ALU.is_gt), 1)
                cv = cx.inc("sv", nc.vector.tensor_copy(x1bf[:, :], zn[:, :]), 1)
                for g in range(8):
                    gb = g % 2
                    cx.wt(nc.tensor, cv, ptg_free[gb])
                    tt_ = Z
                    for k in range(4):
                        tt_ = cx.inc("sm", nc.tensor.transpose(
                            pt4[:, 1024 * gb + 128 * k:1024 * gb + 128 * k + 128],
                            x1bf[:, 128 * (4 * g + k):128 * (4 * g + k) + 128], ident_s[:, :]), 1)
                    cx.wt(nc.vector, tt_)
                    dst = x1T.ap().rearrange("p (k t) -> p k t", t=1024)
                    ev = cx.inc("sv", nc.vector.tensor_copy(
                        dst[:, 4 * g:4 * g + 4, 128 * tt:128 * tt + 128],
                        pt4.ap().rearrange("p (k t) -> p k t", t=128)[:, 8 * gb:8 * gb + 4, :]), 1)
                    ptg_free[gb] = ev
            if "d_x1" in dbg_t:
                cx.wait(nc.sync, "sx1sp", "sv")
                cx.dma(nc.sync, dbg_t["d_x1"][:, :], x1_sp[:, :], sem="sp")

        with (
            nc.sbuf_tensor("fm0", [128, 32 * 512], BF16) as fm0,
            nc.sbuf_tensor("fm1", [128, 32 * 512], BF16) as fm1,
            nc.sbuf_tensor("y4b", [128, 4 * 512], F32) as y4b,
            nc.sbuf_tensor("ysel", [128, 512], F32) as ysel,
            nc.psum_tensor("pf0", [128, 512], F32) as pf0,
            nc.psum_tensor("pf1", [128, 512], F32) as pf1,
        ):
            bias_broadcast(fb_r[:, :], bias_bc, D, pf0)
            cx.wait(nc.sync, "scc", "sc", "sx1sp")
            x1T_tok = cx.tok("sv")
            fms = [fm0, fm1]
            fready = [Z, Z]
            ffree = [Z, Z]
            psums = [pf0, pf1]
            pfree = [Z, Z]
            y4_tok = [Z, Z, Z, Z]
            sel_done = Z
            x1c_tok = Z

            def load_fm(quad, ob, mat, buf):
                base = 8192 * ob + 4096 * mat
                cx.wt(nc.sync, ffree[buf])
                dt = Z
                for j in range(8):
                    dt = cx.dma(nc.sync,
                                fms[buf][:, 2048 * j:2048 * j + 2048].rearrange(
                                    "p (a o) -> p a o", o=512),
                                ag_out[base + 512 * j:base + 512 * j + 512, :].rearrange(
                                    "(a p) o -> p a o", p=128), sem=f"sfm{buf}")
                fready[buf] = dt

            seq = [(q, ob, mat) for q in range(2) for ob in range(8) for mat in range(2)]
            load_fm(*seq[0], 0)
            for i, (quad, ob, mat) in enumerate(seq):
                buf = i % 2
                if i + 1 < len(seq):
                    load_fm(*seq[i + 1], (i + 1) % 2)
                for ti in range(4):
                    tt = 4 * quad + ti
                    g = ti % 2
                    pb = psums[g]
                    cx.wt(nc.tensor, fready[buf], x1T_tok, pfree[g])
                    mt = Z
                    for k in range(32):
                        mt = cx.inc("sm", nc.tensor.matmul(
                            pb[:, :],
                            x1T[:, 1024 * k + 128 * tt: 1024 * k + 128 * tt + 128],
                            fms[buf][:, 512 * k:512 * k + 512],
                            start=(k == 0), stop=(k == 31)), 1)
                    ffree[buf] = mt
                    if mat == 0:
                        cx.wt(nc.vector, mt, y4_tok[ti])
                        ev = cx.inc("sv", nc.vector.tensor_copy(
                            y4b[:, 512 * ti:512 * ti + 512], pb[:, :]), 1)
                        pfree[g] = ev
                    else:
                        cx.wt(nc.sync, x1c_tok)
                        xd = cx.dma(nc.sync, stage[:, 0:512],
                                    x1_sp[128 * tt:128 * tt + 128, 512 * ob:512 * ob + 512],
                                    sem="sxr")
                        cx.wt(nc.vector, mt, sel_done)
                        s1 = cx.inc("sv", nc.vector.tensor_sub(
                            ysel[:, :], pb[:, :], y4b[:, 512 * ti:512 * ti + 512]), 1)
                        pfree[g] = s1
                        s2 = cx.inc("sv", nc.vector.scalar_tensor_tensor(
                            ysel[:, :], ysel[:, :], small[:, 32 + tt:33 + tt],
                            y4b[:, 512 * ti:512 * ti + 512], ALU.mult, ALU.add), 1)
                        y4_tok[ti] = s2
                        cx.wt(nc.vector, xd)
                        x1c_tok = cx.inc("sv", nc.vector.tensor_add(ysel[:, :], ysel[:, :],
                                                                    stage[:, 0:512]), 1)
                        s4 = cx.inc("sv", nc.vector.tensor_add(
                            ysel[:, :], ysel[:, :], bias_bc[:, 512 * ob:512 * ob + 512]), 1)
                        cx.wt(nc.sync, s4)
                        sel_done = cx.dma(nc.sync, z2_sp[128 * tt:128 * tt + 128,
                                                         512 * ob:512 * ob + 512],
                                          ysel[:, :], sem="sz2")
            if "d_z2" in dbg_t:
                cx.wait(nc.sync, "sz2")
                cx.dma(nc.sync, dbg_t["d_z2"][:, :], z2_sp[:, :], sem="sp")

        with (
            nc.sbuf_tensor("zn2", [128, 4096], F32) as zn2,
            nc.sbuf_tensor("red2", [128, 32], F32) as red2,
        ):
            cx.wait(nc.sync, "sz2")
            for tt in range(8):
                cx.wait(nc.sync, "sv", "sc", "sp")
                cx.dma(nc.sync, zn2[:, :], z2_sp[128 * tt:128 * tt + 128, :], sem="srd")
                cx.wait(nc.vector, "srd", "sc")
                _ln_stats(cx, nc, zn2, red2, stage, small[:, 48:49])
                cx.wait(nc.vector, "sc")
                cx.inc("sv", nc.vector.tensor_scalar(zn2[:, :], zn2[:, :], red2[:, 26:27],
                                                     None, ALU.mult), 1)
                cx.wait(nc.sync, "sv")
                cx.dma(nc.sync, out_rows[128 * tt:128 * tt + 128, :], zn2[:, :], sem="sp")


def _ln_stats(cx, nc, zn, red, scratch, eps_ap):
    cx.inc("sv", nc.vector.reduce_sum(red[:, 0:1], zn[:, :],
                                      axis=mybir.AxisListType.X), 1)
    cx.wait(nc.scalar, "sv")
    cx.inc("sc", nc.scalar.mul(red[:, 9:10], red[:, 0:1], -1.0 / D), 1)
    cx.wait(nc.vector, "sc")
    cx.inc("sv", nc.vector.tensor_copy(red[:, 10:11], red[:, 9:10]), 1)
    cx.wait(nc.scalar, "sv")
    cx.inc("sc", nc.scalar.add(zn[:, :], zn[:, :], red[:, 10:11]), 1)
    for ch in range(8):
        cx.inc("sc", nc.scalar.activation(scratch[:, 512 * ch:512 * ch + 512],
                                          zn[:, 512 * ch:512 * ch + 512], AF.Square,
                                          accum_out=red[:, 16 + ch:17 + ch]), 1)
    cx.wait(nc.vector, "sc")
    cx.inc("sv", nc.vector.reduce_sum(red[:, 24:25], red[:, 16:24],
                                      axis=mybir.AxisListType.X), 1)
    cx.wait(nc.scalar, "sv")
    cx.inc("sc", nc.scalar.activation(red[:, 25:26], red[:, 24:25], AF.Ln,
                                      scale=1.0 / D, bias=eps_ap), 1)
    cx.wait(nc.vector, "sc")
    cx.inc("sv", nc.vector.tensor_scalar(red[:, 27:28], red[:, 25:26], -0.5, None, ALU.mult), 1)
    cx.wait(nc.scalar, "sv")
    cx.inc("sc", nc.scalar.activation(red[:, 26:27], red[:, 27:28], AF.Exp), 1)


def _layout_inputs(ins):
    x = np.ascontiguousarray(ins["L0_attn_x"])
    Wq, Wk, Wv, Wo = (np.asarray(ins["L0_attn_Wq"]), np.asarray(ins["L0_attn_Wk"]),
                      np.asarray(ins["L0_attn_Wv"]), np.asarray(ins["L0_attn_Wo"]))
    bq, bk, bv, bo = (np.asarray(ins["L0_attn_bq"]), np.asarray(ins["L0_attn_bk"]),
                      np.asarray(ins["L0_attn_bv"]), np.asarray(ins["L0_attn_bo"]))
    qweight = np.asarray(ins["L0_ffn_qweight"])
    lut4 = np.asarray(ins["L0_ffn_lut4"])
    lut8 = np.asarray(ins["L0_ffn_lut8"])
    fbias = np.asarray(ins["L0_ffn_bias"])

    ident = np.eye(128, dtype=ml_dtypes.bfloat16)
    ones = np.ones((1, 128), dtype=ml_dtypes.bfloat16)
    tri = np.where(np.arange(128)[None, :] > np.arange(128)[:, None],
                   np.float32(NEG), np.float32(0.0)).astype(np.float32)

    maps = []
    for c in range(8):
        b, hh = c // 2, c % 2
        dsl = slice(2048 * hh, 2048 * hh + 2048)
        osl = slice(512 * c, 512 * c + 512)
        qw_core = np.empty((4, 128, 8 * 128), dtype=np.int32)
        for ot in range(4):
            for p in range(8):
                qw_core[ot, :, 128 * p:128 * p + 128] = \
                    qweight[p, 512 * c + 128 * ot:512 * c + 128 * ot + 128, :]
        maps.append({
            "xT": np.ascontiguousarray(x[b].T),
            "x_rows": np.ascontiguousarray(x[b, 1024 * hh:1024 * hh + 1024, :]),
            "wqT": np.ascontiguousarray(Wq[dsl, :].T),
            "wkT": np.ascontiguousarray(Wk[dsl, :].T),
            "wvT": np.ascontiguousarray(Wv[dsl, :].T),
            "woT": np.ascontiguousarray(Wo.T[dsl, :]),
            "bq_t": np.ascontiguousarray(bq[dsl].reshape(16, 128).T),
            "bk_t": np.ascontiguousarray(bk[dsl].reshape(16, 128).T),
            "bv_r": np.ascontiguousarray(bv[dsl].reshape(1, 2048)),
            "bo_r": np.ascontiguousarray(bo.reshape(1, 4096)),
            "fb_r": np.ascontiguousarray(fbias.reshape(1, 4096)),
            "qw32": qw_core,
            "lut4_t": np.ascontiguousarray(
                lut4[osl].reshape(4, 128, 16).transpose(1, 0, 2).reshape(128, 64)),
            "lut8_t": np.ascontiguousarray(
                lut8[osl].reshape(4, 128, 256).transpose(1, 0, 2).reshape(128, 1024)),
            "ident": ident,
            "ones_r": ones,
            "trineg": tri,
        })
    return maps


def kernel(**inputs):
    maps = _layout_inputs(inputs)
    nc = build_program()
    res = run_bass_kernel_spmd(nc, maps, core_ids=list(range(8)))
    out = np.empty((B, L, D), dtype=np.float32)
    for c in range(8):
        b, hh = c // 2, c % 2
        out[b, 1024 * hh:1024 * hh + 1024, :] = res.results[c]["out_rows"]
    return out


# version 03
# speedup vs baseline: 1.7852x; 1.0655x over previous
import math
import numpy as np
import ml_dtypes
from contextlib import ExitStack

import concourse.bass as bass
import concourse.mybir as mybir
from concourse.bass_utils import run_bass_kernel_spmd

F32 = mybir.dt.float32
BF16 = mybir.dt.bfloat16
I32 = mybir.dt.int32
I16 = mybir.dt.int16
AF = mybir.ActivationFunctionType
ALU = mybir.AluOpType

B, L, D = 4, 2048, 4096
HALF = 2048
TOK = 1024
NHEAD = 16
HD = 128
OSH = 512
SCALE = 1.0 / math.sqrt(HD)
NEG = -1.0e30
Z = ("sd", 0)


class Ctx:
    def __init__(self, nc, stack):
        self.nc = nc
        self.sems = {}
        for name in ("sd", "sc", "sv", "sm", "sp", "scc", "si", "sr",
                     "sxa", "sxb", "sw2", "sw3", "sw4", "sw5",
                     "se0", "se1", "svh0", "svh1", "skh0", "skh1",
                     "sqh0", "sqh1", "spo0", "spo1", "sfm0", "sfm1",
                     "sgl", "sgt", "sx1sp", "sz2", "srd", "sxr"):
            self.sems[name] = [stack.enter_context(nc.semaphore(name)), 0]

    def inc(self, name, instr, k):
        s = self.sems[name]
        instr.then_inc(s[0], k)
        s[1] += k
        return (name, s[1])

    def dma(self, eng, out, in_, sem):
        return self.inc(sem, eng.dma_start(out, in_), 16)

    def tok(self, name):
        return (name, self.sems[name][1])

    def wt(self, eng, *tokens):
        best = {}
        for t in tokens:
            if t is None:
                continue
            n, v = t
            if v > 0:
                best[n] = max(best.get(n, 0), v)
        for n, v in best.items():
            eng.wait_ge(self.sems[n][0], v)

    def wait(self, eng, *names):
        for n in names:
            s = self.sems[n]
            if s[1] > 0:
                eng.wait_ge(s[0], s[1])


def build_program(dbg=()):
    nc = bass.Bass("TRN2", target_bir_lowering=False)

    xT = nc.dram_tensor("xT", [D, L], F32, kind="ExternalInput").ap()
    x_rows = nc.dram_tensor("x_rows", [TOK, D], F32, kind="ExternalInput").ap()
    wqT = nc.dram_tensor("wqT", [D, HALF], F32, kind="ExternalInput").ap()
    wkT = nc.dram_tensor("wkT", [D, HALF], F32, kind="ExternalInput").ap()
    wvT = nc.dram_tensor("wvT", [D, HALF], F32, kind="ExternalInput").ap()
    woT = nc.dram_tensor("woT", [HALF, D], F32, kind="ExternalInput").ap()
    bq_t = nc.dram_tensor("bq_t", [128, NHEAD], F32, kind="ExternalInput").ap()
    bk_t = nc.dram_tensor("bk_t", [128, NHEAD], F32, kind="ExternalInput").ap()
    bv_r = nc.dram_tensor("bv_r", [1, HALF], F32, kind="ExternalInput").ap()
    bo_r = nc.dram_tensor("bo_r", [1, D], F32, kind="ExternalInput").ap()
    fb_r = nc.dram_tensor("fb_r", [1, D], F32, kind="ExternalInput").ap()
    qw32 = nc.dram_tensor("qw32", [4, 128, 8 * 128], I32, kind="ExternalInput").ap()
    lut4_t = nc.dram_tensor("lut4_t", [128, 4 * 16], F32, kind="ExternalInput").ap()
    lut8_t = nc.dram_tensor("lut8_t", [128, 4 * 256], F32, kind="ExternalInput").ap()
    ident = nc.dram_tensor("ident", [128, 128], BF16, kind="ExternalInput").ap()
    ones_r = nc.dram_tensor("ones_r", [1, 128], BF16, kind="ExternalInput").ap()
    trineg = nc.dram_tensor("trineg", [128, 128], F32, kind="ExternalInput").ap()

    out_rows = nc.dram_tensor("out_rows", [TOK, D], F32, kind="ExternalOutput").ap()

    qT_sp = nc.dram_tensor("qT_sp", [HALF, L], BF16, kind="Internal").ap()
    kT_sp = nc.dram_tensor("kT_sp", [HALF, L], BF16, kind="Internal").ap()
    v_sp = nc.dram_tensor("v_sp", [NHEAD, L, HD], BF16, kind="Internal").ap()
    x1_sp = nc.dram_tensor("x1_sp", [TOK, D], F32, kind="Internal").ap()
    z2_sp = nc.dram_tensor("z2_sp", [TOK, D], F32, kind="Internal").ap()
    rs_in = [nc.dram_tensor(f"rs_in{i}", [L, 1024], F32, kind="Internal").ap() for i in range(4)]
    rs_out = [nc.dram_tensor(f"rs_out{i}", [TOK, 1024], F32, kind="Internal").ap() for i in range(4)]
    ag_in = nc.dram_tensor("ag_in", [2 * D, OSH], BF16, kind="Internal").ap()
    ag_out = nc.dram_tensor("ag_out", [8 * 2 * D, OSH], BF16, kind="Internal").ap()

    dbg_t = {}
    if "x1" in dbg:
        dbg_t["d_x1"] = nc.dram_tensor("d_x1", [TOK, D], F32, kind="ExternalOutput").ap()
    if "z2" in dbg:
        dbg_t["d_z2"] = nc.dram_tensor("d_z2", [TOK, D], F32, kind="ExternalOutput").ap()
    if "w" in dbg:
        dbg_t["d_w"] = nc.dram_tensor("d_w", [2 * D, OSH], BF16, kind="ExternalOutput").ap()
    if "qkv" in dbg:
        dbg_t["d_qT"] = nc.dram_tensor("d_qT", [HALF, L], BF16, kind="ExternalOutput").ap()
        dbg_t["d_kT"] = nc.dram_tensor("d_kT", [HALF, L], BF16, kind="ExternalOutput").ap()
        dbg_t["d_v"] = nc.dram_tensor("d_v", [16, L, 128], BF16, kind="ExternalOutput").ap()
    if "ao" in dbg:
        dbg_t["d_ao"] = nc.dram_tensor("d_ao", [HALF, L], BF16, kind="ExternalOutput").ap()

    stack = ExitStack()
    with stack:
        cx = Ctx(nc, stack)
        stage = stack.enter_context(nc.sbuf_tensor("stage", [128, 4096], F32))
        ident_s = stack.enter_context(nc.sbuf_tensor("ident_s", [128, 128], BF16))
        ones_s = stack.enter_context(nc.sbuf_tensor("ones_s", [1, 128], BF16))
        trineg_s = stack.enter_context(nc.sbuf_tensor("trineg_s", [128, 128], F32))
        small = stack.enter_context(nc.sbuf_tensor("small", [128, 64], F32))
        rowbuf = stack.enter_context(nc.sbuf_tensor("rowbuf", [1, 512], F32))
        rowbf = stack.enter_context(nc.sbuf_tensor("rowbf", [1, 512], BF16))

        cx.dma(nc.sync, ident_s[:, :], ident[:, :], sem="si")
        cx.dma(nc.sync, ones_s[:, :], ones_r[:, :], sem="si")
        cx.dma(nc.sync, trineg_s[:, :], trineg[:, :], sem="si")
        cx.dma(nc.sync, small[:, 0:NHEAD], bq_t[:, :], sem="si")
        cx.dma(nc.sync, small[:, 16:16 + NHEAD], bk_t[:, :], sem="si")
        cx.inc("sv", nc.vector.memset(small[:, 48:49], 1e-5), 1)
        cx.wait(nc.tensor, "si")
        cx.wait(nc.scalar, "si")

        def bias_broadcast(row_dram, dst_sbuf, width, psum):
            for j in range(width // 512):
                cx.wait(nc.sync, "sv", "sm")
                cx.dma(nc.sync, rowbuf[0:1, 0:512], row_dram[0:1, 512 * j:512 * j + 512],
                       sem="sr")
                cx.wait(nc.vector, "sr", "sm", "si")
                cx.inc("sv", nc.vector.tensor_copy(rowbf[0:1, 0:512], rowbuf[0:1, 0:512]), 1)
                cx.wait(nc.tensor, "sv")
                cx.inc("sm", nc.tensor.matmul(psum[:, 0:512], ones_s[:, :],
                                              rowbf[0:1, 0:512], start=True, stop=True), 1)
                cx.wait(nc.vector, "sm")
                cx.inc("sv", nc.vector.tensor_copy(dst_sbuf[:, 512 * j:512 * j + 512],
                                                   psum[:, 0:512]), 1)

        with (
            nc.sbuf_tensor("qw_s", [128, 1024], I32) as qw_s,
            nc.sbuf_tensor("idx16", [128, 4096], I16) as idx16,
            nc.sbuf_tensor("idx4h", [128, 2048], I16) as idx4h,
            nc.sbuf_tensor("accI", [128, 4096], I32) as accI,
            nc.sbuf_tensor("tmpI", [128, 128], I32) as tmpI,
            nc.sbuf_tensor("lut4_s", [128, 64], F32) as lut4_s,
            nc.sbuf_tensor("lut8_s", [128, 256], F32) as lut8_s,
            nc.sbuf_tensor("wtmp", [128, 2048], BF16) as wtmp,
            nc.sbuf_tensor("wacc", [128, 2048], BF16) as wacc,
            nc.sbuf_tensor("ttile", [128, 512], BF16) as ttile,
            nc.psum_tensor("pt_deq", [128, 1024], BF16) as pt_deq,
        ):
            deq = _dequant_gen(cx, nc, qw32, lut4_t, lut8_t, ag_in, ag_out, ident_s,
                               qw_s, idx16, idx4h, accI, tmpI, lut4_s, lut8_s,
                               wtmp, wacc, ttile, pt_deq, dbg_t)

            def pump(n):
                for _ in range(n):
                    if next(deq, "DONE") == "DONE":
                        return

            emit_proj(cx, nc, xT, wqT, wkT, wvT, bv_r, kT_sp, qT_sp, v_sp,
                      stage, small, bias_broadcast, pump)
            if "d_qT" in dbg_t:
                cx.wait(nc.sync, "se0", "se1", "sv", "sc")
                cx.dma(nc.sync, dbg_t["d_qT"][:, :], qT_sp[:, :], sem="sp")
                cx.dma(nc.sync, dbg_t["d_kT"][:, :], kT_sp[:, :], sem="sp")
                cx.dma(nc.sync, dbg_t["d_v"][:, :, :], v_sp[:, :, :], sem="sp")
            with nc.sbuf_tensor("aoT", [128, 16 * L], BF16) as aoT:
                emit_attn(cx, nc, kT_sp, aoT, qT_sp, v_sp, ident_s, trineg_s, pump)
                if "d_ao" in dbg_t:
                    cx.wait(nc.sync, "sv", "sm")
                    for k in range(16):
                        cx.dma(nc.sync, dbg_t["d_ao"][128 * k:128 * k + 128, :],
                               aoT[:, L * k:L * k + L], sem="sp")
                    cx.wait(nc.sync, "sp")
                emit_wo_rs(cx, nc, aoT, woT, stage, rs_in, rs_out, pump)
            pump(10 ** 9)

        emit_ln1_ffn_ln2(cx, nc, x_rows, x1_sp, z2_sp, rs_out, ag_out, bo_r, fb_r,
                         out_rows, stage, small, ident_s, bias_broadcast, dbg_t)
        cx.wait(nc.sync, "sp")
    return nc


def _dequant_gen(cx, nc, qw32, lut4_t, lut8_t, ag_in, ag_out, ident_s,
                 qw_s, idx16, idx4h, accI, tmpI, lut4_s, lut8_s,
                 wtmp, wacc, ttile, pt_deq, dbg_t):
    cx.dma(nc.sync, lut4_s[:, :], lut4_t[:, :], sem="si")
    jl_tok = Z
    tp_tok = Z
    ttile_dma = Z

    ev_prev = Z

    def jloop(nj, lut_s, base, dst0, idx_ap, i0, ot):
        nonlocal jl_tok, tp_tok, ttile_dma, ev_prev
        cx.wt(nc.vector, tp_tok)
        for j in range(nj):
            col = lut_s[:, base + j:base + j + 1]
            tgt = wacc if j == 0 else wtmp
            jl_tok = cx.inc("sv", nc.vector.tensor_scalar(
                tgt[:, :], idx_ap, int(j), col,
                ALU.is_equal, ALU.mult), 1)
            if j > 0:
                jl_tok = cx.inc("sv", nc.vector.tensor_add(
                    wacc[:, :], wacc[:, :], wtmp[:, :]), 1)
            if j % 16 == 15:
                yield
        for g in range(4):
            cx.wt(nc.tensor, jl_tok, ttile_dma, ev_prev)
            for k in range(4):
                kk = 4 * g + k
                tp_tok = cx.inc("sm", nc.tensor.transpose(
                    pt_deq[:, 128 * k:128 * k + 128],
                    wacc[:, 128 * kk:128 * kk + 128], ident_s[:, :]), 1)
            cx.wt(nc.vector, tp_tok, ttile_dma)
            ev = cx.inc("sv", nc.vector.tensor_copy(ttile[:, :], pt_deq[:, 0:512]), 1)
            ev_prev = ev
            cx.wt(nc.gpsimd, ev)
            for k in range(4):
                kk = 4 * g + k
                ttile_dma = cx.dma(
                    nc.gpsimd,
                    ag_in[dst0 + i0 + 128 * kk: dst0 + i0 + 128 * kk + 128,
                          128 * ot:128 * ot + 128],
                    ttile[:, 128 * k:128 * k + 128], sem="sgt")
            if g % 2 == 1:
                yield

    for ot in range(4):
        cx.wt(nc.gpsimd, jl_tok)
        qw_tok = cx.dma(nc.gpsimd, qw_s[:, :], qw32[ot, :, :], sem="sgl")
        cx.wt(nc.gpsimd, jl_tok)
        l8tok = cx.dma(nc.gpsimd, lut8_s[:, :], lut8_t[:, 256 * ot:256 * ot + 256], sem="sgl")
        cx.wt(nc.vector, qw_tok, l8tok, cx.tok("si") if ot == 0 else None)
        acc3 = accI.ap().rearrange("p (w x) -> p w x", x=32)
        for s in range(32):
            for p in range(8):
                if p == 0:
                    jl_tok = cx.inc("sv", nc.vector.tensor_scalar(
                        acc3[:, :, s], qw_s[:, 0:128], int(s), 1,
                        ALU.logical_shift_right, ALU.bitwise_and), 1)
                else:
                    srcp = qw_s[:, 128 * p:128 * p + 128]
                    if s >= p:
                        jl_tok = cx.inc("sv", nc.vector.tensor_scalar(
                            tmpI[:, :], srcp, int(s - p), int(1 << p),
                            ALU.logical_shift_right, ALU.bitwise_and), 1)
                    else:
                        jl_tok = cx.inc("sv", nc.vector.tensor_scalar(
                            tmpI[:, :], srcp, int(p - s), int(1 << p),
                            ALU.logical_shift_left, ALU.bitwise_and), 1)
                    jl_tok = cx.inc("sv", nc.vector.tensor_tensor(
                        acc3[:, :, s], acc3[:, :, s], tmpI[:, :], ALU.bitwise_or), 1)
            if s % 8 == 7:
                yield
        jl_tok = cx.inc("sv", nc.vector.tensor_copy(idx16[:, :], accI[:, :]), 1)
        yield
        for ih in range(2):
            i0 = 2048 * ih
            jl_tok = cx.inc("sv", nc.vector.tensor_single_scalar(
                idx4h[:, :], idx16[:, i0:i0 + 2048], 15, ALU.bitwise_and), 1)
            yield from jloop(16, lut4_s, 16 * ot, 0, idx4h[:, :], i0, ot)
            yield from jloop(256, lut8_s, 0, D, idx16[:, i0:i0 + 2048], i0, ot)
    cx.wt(nc.gpsimd, ttile_dma)
    cx.inc("scc", nc.gpsimd.collective_compute(
        "AllGather", ALU.bypass,
        replica_groups=[[0, 1, 2, 3, 4, 5, 6, 7]],
        ins=[ag_in[:, :]], outs=[ag_out[:, :]]), 1)
    if "d_w" in dbg_t:
        cx.wait(nc.sync, "sgt")
        cx.dma(nc.sync, dbg_t["d_w"][:, :], ag_in[:, :], sem="sp")


def emit_proj(cx, nc, xT, wqT, wkT, wvT, bv_r, kT_sp, qT_sp, v_sp,
              stage, small, bias_broadcast, pump):
    with (
        nc.sbuf_tensor("xt_h", [128, 32 * 1024], BF16) as xt_h,
        nc.sbuf_tensor("wbuf0", [128, 32 * 128], BF16) as wbuf0,
        nc.sbuf_tensor("wbuf1", [128, 32 * 128], BF16) as wbuf1,
        nc.sbuf_tensor("vbias", [128, HALF], BF16) as vbias,
        nc.sbuf_tensor("evq0", [128, 512], BF16) as evq0,
        nc.sbuf_tensor("evq1", [128, 512], BF16) as evq1,
        nc.psum_tensor("pp0", [128, 512], F32) as pp0,
        nc.psum_tensor("pp1", [128, 512], F32) as pp1,
    ):
        bias_broadcast(bv_r[:, :], vbias, HALF, pp0)
        vbias_tok = cx.tok("sv")
        wbufs = [wbuf0, wbuf1]
        wfree = [Z, Z]
        wready = [Z, Z]
        psums = [pp0, pp1]
        pfree = [Z, Z]
        evqs = [evq0, evq1]
        efree = [Z, Z]
        stg = [Z] * 6

        def load_w_16th(wsrc, os_, buf):
            o0 = 128 * os_
            ctok = Z
            for g in range(8):
                sl = 2 + g % 2
                c0 = 1024 + 512 * (g % 2)
                cx.wt(nc.sync, stg[sl], wfree[buf] if g == 0 else None)
                dtok = cx.dma(
                    nc.sync,
                    stage[:, c0:c0 + 512].rearrange("p (a o) -> p a o", o=128),
                    wsrc[512 * g:512 * g + 512, o0:o0 + 128].rearrange(
                        "(a p) o -> p a o", p=128),
                    sem=f"sw{sl}")
                cx.wt(nc.scalar, dtok, wfree[buf] if g == 0 else None)
                ctok = cx.inc("sc", nc.scalar.copy(
                    wbufs[buf][:, 512 * g:512 * g + 512], stage[:, c0:c0 + 512]), 1)
                stg[sl] = ctok
            wready[buf] = ctok

        for th in range(2):
            t0 = 1024 * th
            xt_last = Z
            for k in range(32):
                sl = k % 2
                xsem = "sxa" if sl == 0 else "sxb"
                cx.wt(nc.sync, stg[sl])
                cx.dma(nc.sync, stage[:, 512 * sl:512 * sl + 512],
                       xT[128 * k:128 * k + 128, t0:t0 + 512], sem=xsem)
                dtok2 = cx.dma(nc.sync, stage[:, 512 * sl + 2048:512 * sl + 2048 + 512],
                               xT[128 * k:128 * k + 128, t0 + 512:t0 + 1024], sem=xsem)
                cx.wt(nc.vector, dtok2, wfree[0] if k == 0 else None,
                      wfree[1] if k == 0 else None)
                cx.inc("sv", nc.vector.tensor_copy(
                    xt_h[:, 1024 * k:1024 * k + 512], stage[:, 512 * sl:512 * sl + 512]), 1)
                ctok = cx.inc("sv", nc.vector.tensor_copy(
                    xt_h[:, 1024 * k + 512:1024 * k + 1024],
                    stage[:, 512 * sl + 2048:512 * sl + 2048 + 512]), 1)
                stg[sl] = ctok
                xt_last = ctok
            pump(3)
            for (wsrc, kind) in ((wkT, "k"), (wvT, "v"), (wqT, "q")):
                load_w_16th(wsrc, 0, 0)
                for os_ in range(16):
                    buf = os_ % 2
                    if os_ < 15:
                        load_w_16th(wsrc, os_ + 1, (os_ + 1) % 2)
                    g = 0
                    og = os_
                    if kind in ("q", "k"):
                        for tb in range(2):
                            pb = psums[g % 2]
                            cx.wt(nc.tensor, wready[buf], xt_last, pfree[g % 2])
                            mt = Z
                            for k in range(32):
                                mt = cx.inc("sm", nc.tensor.matmul(
                                    pb[:, :],
                                    wbufs[buf][:, 128 * k:128 * k + 128],
                                    xt_h[:, 1024 * k + 512 * tb: 1024 * k + 512 * tb + 512],
                                    start=(k == 0), stop=(k == 31)), 1)
                            wfree[buf] = mt
                            bcol = (0 if kind == "q" else 16) + og
                            spt = qT_sp if kind == "q" else kT_sp
                            cx.wt(nc.scalar, mt, efree[g % 2])
                            ev = cx.inc("sc", nc.scalar.add(
                                evqs[g % 2][:, :], pb[:, :], small[:, bcol:bcol + 1]), 1)
                            cx.wt(nc.sync, ev)
                            efree[g % 2] = cx.dma(
                                nc.sync, spt[128 * og:128 * og + 128,
                                             t0 + 512 * tb: t0 + 512 * tb + 512],
                                evqs[g % 2][:, :], sem=f"se{g % 2}")
                            pfree[g % 2] = ev
                            g += 1
                        if os_ % 2 == 1:
                            pump(1)
                    else:
                        for tt in range(8):
                            pb = psums[g % 2]
                            cx.wt(nc.tensor, wready[buf], xt_last, pfree[g % 2])
                            mt = Z
                            for k in range(32):
                                mt = cx.inc("sm", nc.tensor.matmul(
                                    pb[:, 0:128],
                                    xt_h[:, 1024 * k + 128 * tt: 1024 * k + 128 * tt + 128],
                                    wbufs[buf][:, 128 * k:128 * k + 128],
                                    start=(k == 0), stop=(k == 31)), 1)
                            wfree[buf] = mt
                            cx.wt(nc.vector, mt, vbias_tok, efree[g % 2])
                            ev = cx.inc("sv", nc.vector.tensor_add(
                                evqs[g % 2][:, 0:128], pb[:, 0:128],
                                vbias[:, 128 * os_:128 * os_ + 128]), 1)
                            cx.wt(nc.sync, ev)
                            efree[g % 2] = cx.dma(
                                nc.sync, v_sp[os_, t0 + 128 * tt: t0 + 128 * tt + 128, :],
                                evqs[g % 2][:, 0:128], sem=f"se{g % 2}")
                            pfree[g % 2] = ev
                            g += 1
                        if tt == 7 and os_ % 2 == 1:
                            pump(1)


def emit_attn(cx, nc, kT_sp, aoT, qT_sp, v_sp, ident_s, trineg_s, pump):
    kq_tok = cx.tok("sc")
    sp_q0, sp_q1 = cx.tok("se0"), cx.tok("se1")
    with (
        nc.sbuf_tensor("kh0", [128, L], BF16) as kh0,
        nc.sbuf_tensor("kh1", [128, L], BF16) as kh1,
        nc.sbuf_tensor("vh0", [128, 16 * 128], BF16) as vh0,
        nc.sbuf_tensor("vh1", [128, 16 * 128], BF16) as vh1,
        nc.sbuf_tensor("qh0", [128, 512], BF16) as qh0,
        nc.sbuf_tensor("qh1", [128, 512], BF16) as qh1,
        nc.sbuf_tensor("ebf0", [128, 2048], BF16) as ebf0,
        nc.sbuf_tensor("ebf1", [128, 2048], BF16) as ebf1,
        nc.sbuf_tensor("etb", [128, 16 * 512], BF16) as etb,
        nc.sbuf_tensor("zrow", [128, 16], F32) as zrow,
        nc.psum_tensor("ps_s", [128, 4 * 512], F32) as ps_s,
        nc.psum_tensor("ps_t", [128, 2048], BF16) as ps_t,
        nc.psum_tensor("ps_av", [128, 512], F32) as ps_av,
    ):
        vhs, qhs, khs = [vh0, vh1], [qh0, qh1], [kh0, kh1]
        ebfs = [ebf0, ebf1]
        vh_tok = [Z, Z]
        vh_free = [Z, Z]
        kh_tok = [Z, Z]
        kh_free = [Z, Z]
        qh_tok = [Z, Z]
        qh_free = [Z, Z]
        exp1_done = Z
        tp_free = [Z, Z]
        et_ev = Z
        et_ms = Z
        av_tok = Z
        ao_ev = Z
        ptg_free = [Z, Z]
        it = 0
        vh3 = [v.ap().rearrange("p (k d) -> p k d", d=128) for v in vhs]
        cx.wt(nc.sync, kq_tok, sp_q0, sp_q1)
        vh_tok[0] = cx.dma(nc.sync, vh3[0][:, :, :],
                           v_sp[0, :, :].rearrange("(k p) d -> p k d", p=128), sem="svh0")
        kh_tok[0] = cx.dma(nc.sync, khs[0][:, :], kT_sp[0:128, :], sem="skh0")
        for h in range(NHEAD):
            hb = h % 2
            if h + 1 < NHEAD:
                cx.wt(nc.sync, vh_free[(h + 1) % 2])
                vh_tok[(h + 1) % 2] = cx.dma(nc.sync, vh3[(h + 1) % 2][:, :, :],
                                             v_sp[h + 1, :, :].rearrange("(k p) d -> p k d", p=128),
                                             sem=f"svh{(h + 1) % 2}")
                cx.wt(nc.sync, kh_free[(h + 1) % 2])
                kh_tok[(h + 1) % 2] = cx.dma(nc.sync, khs[(h + 1) % 2][:, :],
                                             kT_sp[128 * (h + 1):128 * (h + 1) + 128, :],
                                             sem=f"skh{(h + 1) % 2}")
            for sb in range(4):
                qb = sb % 2
                cx.wt(nc.sync, qh_free[qb])
                qh_tok[qb] = cx.dma(nc.sync, qhs[qb][:, :],
                                    qT_sp[128 * h:128 * h + 128, 512 * sb:512 * sb + 512],
                                    sem=f"sqh{qb}")
                for ti in range(4):
                    ib = it % 2
                    it += 1
                    tq0 = 512 * sb + 128 * ti
                    ext = tq0 + 128
                    cx.wt(nc.tensor, exp1_done, qh_tok[qb], kh_tok[hb])
                    mm_tok = Z
                    for tkb in range(sb + 1):
                        n = 512 if tkb < sb else 128 * (ti + 1)
                        mm_tok = cx.inc("sm", nc.tensor.matmul(
                            ps_s[:, 512 * tkb:512 * tkb + n],
                            qhs[qb][:, 128 * ti:128 * ti + 128],
                            khs[hb][:, 512 * tkb: 512 * tkb + n],
                            start=True, stop=True), 1)
                    qh_free[qb] = mm_tok
                    kh_free[hb] = mm_tok
                    cx.wt(nc.vector, mm_tok)
                    msk = cx.inc("sv", nc.vector.tensor_add(
                        ps_s[:, ext - 128:ext], ps_s[:, ext - 128:ext], trineg_s[:, :]), 1)
                    cx.wt(nc.scalar, msk, tp_free[ib])
                    nch = (ext + 511) // 512
                    for ch in range(nch):
                        e0 = 512 * ch
                        e1 = min(ext, e0 + 512)
                        exp1_done = cx.inc("sc", nc.scalar.activation(
                            ebfs[ib][:, e0:e1], ps_s[:, e0:e1], AF.Exp, scale=SCALE,
                            accum_out=zrow[:, 4 * ib + ch:4 * ib + ch + 1]), 1)
                    cx.wt(nc.vector, exp1_done)
                    red = cx.inc("sv", nc.vector.reduce_sum(
                        zrow[:, 8 + ib:9 + ib], zrow[:, 4 * ib:4 * ib + nch],
                        axis=mybir.AxisListType.X), 1)
                    cx.wt(nc.scalar, red)
                    lnt = cx.inc("sc", nc.scalar.activation(
                        zrow[:, 10 + ib:11 + ib], zrow[:, 8 + ib:9 + ib], AF.Ln), 1)
                    cx.wt(nc.vector, lnt)
                    neg = cx.inc("sv", nc.vector.tensor_scalar(
                        zrow[:, 12 + ib:13 + ib], zrow[:, 10 + ib:11 + ib],
                        -1.0, None, ALU.mult), 1)
                    cx.wt(nc.scalar, neg)
                    inv = cx.inc("sc", nc.scalar.activation(
                        zrow[:, 14 + ib:15 + ib], zrow[:, 12 + ib:13 + ib], AF.Exp), 1)
                    cx.wt(nc.vector, inv)
                    ebm = cx.inc("sv", nc.vector.tensor_scalar(
                        ebfs[ib][:, 0:ext], ebfs[ib][:, 0:ext],
                        zrow[:, 14 + ib:15 + ib], None, ALU.mult), 1)
                    ntk = (ext + 127) // 128
                    cx.wt(nc.vector, av_tok)
                    for k in range(ntk, 4 * (sb + 1)):
                        et_ms = cx.inc("sv", nc.vector.memset(
                            etb[:, 512 * k + 128 * ti: 512 * k + 128 * ti + 128], 0.0), 1)
                    for g in range((ntk + 3) // 4):
                        gb = g % 2
                        k0 = 4 * g
                        kn = min(4, ntk - k0)
                        cx.wt(nc.tensor, ebm, ptg_free[gb])
                        tt_ = Z
                        for k in range(kn):
                            tt_ = cx.inc("sm", nc.tensor.transpose(
                                ps_t[:, 1024 * gb + 128 * k:1024 * gb + 128 * k + 128],
                                ebfs[ib][:, 128 * (k0 + k):128 * (k0 + k) + 128], ident_s[:, :]), 1)
                        tp_free[ib] = tt_
                        cx.wt(nc.vector, tt_, av_tok)
                        dst = etb.ap().rearrange("p (k t) -> p k t", t=512)
                        ev = cx.inc("sv", nc.vector.tensor_copy(
                            dst[:, k0:k0 + kn, 128 * ti:128 * ti + 128],
                            ps_t.ap().rearrange("p (k t) -> p k t", t=128)[:, 8 * gb:8 * gb + kn, :]), 1)
                        ptg_free[gb] = ev
                        et_ev = ev
                    pump(1)
                cx.wt(nc.tensor, et_ev, et_ms, vh_tok[hb], ao_ev)
                ntk_sb = 4 * (sb + 1)
                for k in range(ntk_sb):
                    av_tok = cx.inc("sm", nc.tensor.matmul(
                        ps_av[:, :], vhs[hb][:, 128 * k:128 * k + 128],
                        etb[:, 512 * k:512 * k + 512],
                        start=(k == 0), stop=(k == ntk_sb - 1)), 1)
                vh_free[hb] = av_tok
                cx.wt(nc.vector, av_tok)
                ao_ev = cx.inc("sv", nc.vector.tensor_copy(
                    aoT[:, L * h + 512 * sb: L * h + 512 * sb + 512], ps_av[:, :]), 1)
                pump(1)
            pump(3)


def emit_wo_rs(cx, nc, aoT, woT, stage, rs_in, rs_out, pump):
    ao_done = cx.tok("sv")
    with (
        nc.sbuf_tensor("wo0", [128, 16 * 512], BF16) as wo0,
        nc.sbuf_tensor("wo1", [128, 16 * 512], BF16) as wo1,
        nc.sbuf_tensor("poev0", [128, 512], F32) as poev0,
        nc.sbuf_tensor("poev1", [128, 512], F32) as poev1,
        nc.psum_tensor("pw0", [128, 512], F32) as pw0,
        nc.psum_tensor("pw1", [128, 512], F32) as pw1,
    ):
        wos = [wo0, wo1]
        wready = [Z, Z]
        wfree = [Z, Z]
        poevs = [poev0, poev1]
        pofree = [Z, Z]
        pfree = [Z, Z]
        psums = [pw0, pw1]
        stg = [cx.tok("sc")] * 4

        def load_wo(ob, buf):
            ctok = Z
            o0 = 512 * ob
            for k in range(16):
                sl = k % 4
                cx.wt(nc.sync, stg[sl], wfree[buf] if k == 0 else None)
                dtok = cx.dma(nc.sync, stage[:, 1024 * sl:1024 * sl + 512],
                              woT[128 * k:128 * k + 128, o0:o0 + 512], sem=f"sw{2 + sl}")
                cx.wt(nc.scalar, dtok, wfree[buf] if k == 0 else None)
                ctok = cx.inc("sc", nc.scalar.copy(
                    wos[buf][:, 512 * k:512 * k + 512], stage[:, 1024 * sl:1024 * sl + 512]), 1)
                stg[sl] = ctok
            wready[buf] = ctok

        load_wo(0, 0)
        for ob in range(8):
            buf = ob % 2
            if ob < 7:
                load_wo(ob + 1, (ob + 1) % 2)
            for tt in range(16):
                g = tt % 2
                pb = psums[g]
                cx.wt(nc.tensor, wready[buf], ao_done, pfree[g])
                mt = Z
                for k in range(16):
                    mt = cx.inc("sm", nc.tensor.matmul(
                        pb[:, :],
                        aoT[:, L * k + 128 * tt: L * k + 128 * tt + 128],
                        wos[buf][:, 512 * k:512 * k + 512],
                        start=(k == 0), stop=(k == 15)), 1)
                wfree[buf] = mt
                cx.wt(nc.vector, mt, pofree[g])
                ev = cx.inc("sv", nc.vector.tensor_copy(poevs[g][:, :], pb[:, :]), 1)
                pfree[g] = ev
                cx.wt(nc.sync, ev)
                pofree[g] = cx.dma(nc.sync, rs_in[ob // 2][128 * tt:128 * tt + 128,
                                                          512 * (ob % 2):512 * (ob % 2) + 512],
                                   poevs[g][:, :], sem=f"spo{g}")
            pump(2)
            if ob % 2 == 1:
                c4 = ob // 2
                cx.wt(nc.gpsimd, pofree[0], pofree[1])
                cx.inc("scc", nc.gpsimd.collective_compute(
                    "ReduceScatter", ALU.add,
                    replica_groups=[[0, 1], [2, 3], [4, 5], [6, 7]],
                    ins=[rs_in[c4][:, :]], outs=[rs_out[c4][:, :]]), 1)


def emit_ln1_ffn_ln2(cx, nc, x_rows, x1_sp, z2_sp, rs_out, ag_out, bo_r, fb_r,
                     out_rows, stage, small, ident_s, bias_broadcast, dbg_t):
    with (
        nc.sbuf_tensor("x1T", [128, 32 * 1024], BF16) as x1T,
        nc.sbuf_tensor("bias_bc", [128, D], F32) as bias_bc,
    ):
        with (
            nc.sbuf_tensor("zn", [128, 4096], F32) as zn,
            nc.sbuf_tensor("x1bf", [128, 4096], BF16) as x1bf,
            nc.sbuf_tensor("red", [128, 32], F32) as red,
            nc.psum_tensor("pb0", [128, 512], F32) as pb0,
            nc.psum_tensor("pt4", [128, 2048], BF16) as pt4,
        ):
            bias_broadcast(bo_r[:, :], bias_bc, D, pb0)
            cx.wait(nc.sync, "scc")
            ptg_free = [Z, Z]
            for tt in range(8):
                cx.wait(nc.sync, "sv", "sc", "sm", "sx1sp")
                for c4 in range(4):
                    cx.dma(nc.sync, zn[:, 1024 * c4:1024 * c4 + 1024],
                           rs_out[c4][128 * tt:128 * tt + 128, :], sem="srd")
                cx.dma(nc.sync, stage[:, 0:4096], x_rows[128 * tt:128 * tt + 128, :], sem="srd")
                cx.wait(nc.vector, "srd", "sc", "sm", "sx1sp")
                cx.inc("sv", nc.vector.tensor_add(zn[:, :], zn[:, :], stage[:, 0:4096]), 1)
                cx.inc("sv", nc.vector.tensor_add(zn[:, :], zn[:, :], bias_bc[:, :]), 1)
                _ln_stats(cx, nc, zn, red, stage, small[:, 48:49])
                cx.wait(nc.vector, "sc")
                cx.inc("sv", nc.vector.tensor_scalar(zn[:, :], zn[:, :], red[:, 26:27], None, ALU.mult), 1)
                cx.wait(nc.sync, "sv")
                cx.dma(nc.sync, x1_sp[128 * tt:128 * tt + 128, :], zn[:, :], sem="sx1sp")
                cx.wait(nc.scalar, "sv")
                for ch in range(8):
                    cx.inc("sc", nc.scalar.activation(stage[:, 512 * ch:512 * ch + 512],
                                                      zn[:, 512 * ch:512 * ch + 512], AF.Square,
                                                      accum_out=red[:, 16 + ch:17 + ch]), 1)
                cx.wait(nc.vector, "sc")
                cx.inc("sv", nc.vector.reduce_sum(red[:, 30:31], red[:, 16:24],
                                                  axis=mybir.AxisListType.X), 1)
                cx.wait(nc.scalar, "sv")
                cx.inc("sc", nc.scalar.mul(red[:, 31:32], red[:, 30:31], 1.0), 1)
                cx.wait(nc.vector, "sc")
                cx.inc("sv", nc.vector.tensor_scalar(small[:, 32 + tt:33 + tt], red[:, 31:32],
                                                     float(D), None, ALU.is_gt), 1)
                cv = cx.inc("sv", nc.vector.tensor_copy(x1bf[:, :], zn[:, :]), 1)
                for g in range(8):
                    gb = g % 2
                    cx.wt(nc.tensor, cv, ptg_free[gb])
                    tt_ = Z
                    for k in range(4):
                        tt_ = cx.inc("sm", nc.tensor.transpose(
                            pt4[:, 1024 * gb + 128 * k:1024 * gb + 128 * k + 128],
                            x1bf[:, 128 * (4 * g + k):128 * (4 * g + k) + 128], ident_s[:, :]), 1)
                    cx.wt(nc.vector, tt_)
                    dst = x1T.ap().rearrange("p (k t) -> p k t", t=1024)
                    ev = cx.inc("sv", nc.vector.tensor_copy(
                        dst[:, 4 * g:4 * g + 4, 128 * tt:128 * tt + 128],
                        pt4.ap().rearrange("p (k t) -> p k t", t=128)[:, 8 * gb:8 * gb + 4, :]), 1)
                    ptg_free[gb] = ev
            if "d_x1" in dbg_t:
                cx.wait(nc.sync, "sx1sp", "sv")
                cx.dma(nc.sync, dbg_t["d_x1"][:, :], x1_sp[:, :], sem="sp")

        with (
            nc.sbuf_tensor("fm0", [128, 32 * 512], BF16) as fm0,
            nc.sbuf_tensor("fm1", [128, 32 * 512], BF16) as fm1,
            nc.sbuf_tensor("y4b", [128, 4 * 512], F32) as y4b,
            nc.sbuf_tensor("ysel", [128, 512], F32) as ysel,
            nc.psum_tensor("pf0", [128, 512], F32) as pf0,
            nc.psum_tensor("pf1", [128, 512], F32) as pf1,
        ):
            bias_broadcast(fb_r[:, :], bias_bc, D, pf0)
            cx.wait(nc.sync, "scc", "sc", "sx1sp")
            x1T_tok = cx.tok("sv")
            fms = [fm0, fm1]
            fready = [Z, Z]
            ffree = [Z, Z]
            psums = [pf0, pf1]
            pfree = [Z, Z]
            y4_tok = [Z, Z, Z, Z]
            sel_done = Z
            x1c_tok = Z

            def load_fm(quad, ob, mat, buf):
                base = 8192 * ob + 4096 * mat
                cx.wt(nc.sync, ffree[buf])
                dt = Z
                for j in range(8):
                    dt = cx.dma(nc.sync,
                                fms[buf][:, 2048 * j:2048 * j + 2048].rearrange(
                                    "p (a o) -> p a o", o=512),
                                ag_out[base + 512 * j:base + 512 * j + 512, :].rearrange(
                                    "(a p) o -> p a o", p=128), sem=f"sfm{buf}")
                fready[buf] = dt

            seq = [(q, ob, mat) for q in range(2) for ob in range(8) for mat in range(2)]
            load_fm(*seq[0], 0)
            for i, (quad, ob, mat) in enumerate(seq):
                buf = i % 2
                if i + 1 < len(seq):
                    load_fm(*seq[i + 1], (i + 1) % 2)
                for ti in range(4):
                    tt = 4 * quad + ti
                    g = ti % 2
                    pb = psums[g]
                    cx.wt(nc.tensor, fready[buf], x1T_tok, pfree[g])
                    mt = Z
                    for k in range(32):
                        mt = cx.inc("sm", nc.tensor.matmul(
                            pb[:, :],
                            x1T[:, 1024 * k + 128 * tt: 1024 * k + 128 * tt + 128],
                            fms[buf][:, 512 * k:512 * k + 512],
                            start=(k == 0), stop=(k == 31)), 1)
                    ffree[buf] = mt
                    if mat == 0:
                        cx.wt(nc.vector, mt, y4_tok[ti])
                        ev = cx.inc("sv", nc.vector.tensor_copy(
                            y4b[:, 512 * ti:512 * ti + 512], pb[:, :]), 1)
                        pfree[g] = ev
                    else:
                        cx.wt(nc.sync, x1c_tok)
                        xd = cx.dma(nc.sync, stage[:, 0:512],
                                    x1_sp[128 * tt:128 * tt + 128, 512 * ob:512 * ob + 512],
                                    sem="sxr")
                        cx.wt(nc.vector, mt, sel_done)
                        s1 = cx.inc("sv", nc.vector.tensor_sub(
                            ysel[:, :], pb[:, :], y4b[:, 512 * ti:512 * ti + 512]), 1)
                        pfree[g] = s1
                        s2 = cx.inc("sv", nc.vector.scalar_tensor_tensor(
                            ysel[:, :], ysel[:, :], small[:, 32 + tt:33 + tt],
                            y4b[:, 512 * ti:512 * ti + 512], ALU.mult, ALU.add), 1)
                        y4_tok[ti] = s2
                        cx.wt(nc.vector, xd)
                        x1c_tok = cx.inc("sv", nc.vector.tensor_add(ysel[:, :], ysel[:, :],
                                                                    stage[:, 0:512]), 1)
                        s4 = cx.inc("sv", nc.vector.tensor_add(
                            ysel[:, :], ysel[:, :], bias_bc[:, 512 * ob:512 * ob + 512]), 1)
                        cx.wt(nc.sync, s4)
                        sel_done = cx.dma(nc.sync, z2_sp[128 * tt:128 * tt + 128,
                                                         512 * ob:512 * ob + 512],
                                          ysel[:, :], sem="sz2")
            if "d_z2" in dbg_t:
                cx.wait(nc.sync, "sz2")
                cx.dma(nc.sync, dbg_t["d_z2"][:, :], z2_sp[:, :], sem="sp")

        with (
            nc.sbuf_tensor("zn2", [128, 4096], F32) as zn2,
            nc.sbuf_tensor("red2", [128, 32], F32) as red2,
        ):
            cx.wait(nc.sync, "sz2")
            for tt in range(8):
                cx.wait(nc.sync, "sv", "sc", "sp")
                cx.dma(nc.sync, zn2[:, :], z2_sp[128 * tt:128 * tt + 128, :], sem="srd")
                cx.wait(nc.vector, "srd", "sc")
                _ln_stats(cx, nc, zn2, red2, stage, small[:, 48:49])
                cx.wait(nc.vector, "sc")
                cx.inc("sv", nc.vector.tensor_scalar(zn2[:, :], zn2[:, :], red2[:, 26:27],
                                                     None, ALU.mult), 1)
                cx.wait(nc.sync, "sv")
                cx.dma(nc.sync, out_rows[128 * tt:128 * tt + 128, :], zn2[:, :], sem="sp")


def _ln_stats(cx, nc, zn, red, scratch, eps_ap):
    cx.inc("sv", nc.vector.reduce_sum(red[:, 0:1], zn[:, :],
                                      axis=mybir.AxisListType.X), 1)
    cx.wait(nc.scalar, "sv")
    cx.inc("sc", nc.scalar.mul(red[:, 9:10], red[:, 0:1], -1.0 / D), 1)
    cx.wait(nc.vector, "sc")
    cx.inc("sv", nc.vector.tensor_copy(red[:, 10:11], red[:, 9:10]), 1)
    cx.wait(nc.scalar, "sv")
    cx.inc("sc", nc.scalar.add(zn[:, :], zn[:, :], red[:, 10:11]), 1)
    for ch in range(8):
        cx.inc("sc", nc.scalar.activation(scratch[:, 512 * ch:512 * ch + 512],
                                          zn[:, 512 * ch:512 * ch + 512], AF.Square,
                                          accum_out=red[:, 16 + ch:17 + ch]), 1)
    cx.wait(nc.vector, "sc")
    cx.inc("sv", nc.vector.reduce_sum(red[:, 24:25], red[:, 16:24],
                                      axis=mybir.AxisListType.X), 1)
    cx.wait(nc.scalar, "sv")
    cx.inc("sc", nc.scalar.activation(red[:, 25:26], red[:, 24:25], AF.Ln,
                                      scale=1.0 / D, bias=eps_ap), 1)
    cx.wait(nc.vector, "sc")
    cx.inc("sv", nc.vector.tensor_scalar(red[:, 27:28], red[:, 25:26], -0.5, None, ALU.mult), 1)
    cx.wait(nc.scalar, "sv")
    cx.inc("sc", nc.scalar.activation(red[:, 26:27], red[:, 27:28], AF.Exp), 1)


def _layout_inputs(ins):
    x = np.ascontiguousarray(ins["L0_attn_x"])
    Wq, Wk, Wv, Wo = (np.asarray(ins["L0_attn_Wq"]), np.asarray(ins["L0_attn_Wk"]),
                      np.asarray(ins["L0_attn_Wv"]), np.asarray(ins["L0_attn_Wo"]))
    bq, bk, bv, bo = (np.asarray(ins["L0_attn_bq"]), np.asarray(ins["L0_attn_bk"]),
                      np.asarray(ins["L0_attn_bv"]), np.asarray(ins["L0_attn_bo"]))
    qweight = np.asarray(ins["L0_ffn_qweight"])
    lut4 = np.asarray(ins["L0_ffn_lut4"])
    lut8 = np.asarray(ins["L0_ffn_lut8"])
    fbias = np.asarray(ins["L0_ffn_bias"])

    ident = np.eye(128, dtype=ml_dtypes.bfloat16)
    ones = np.ones((1, 128), dtype=ml_dtypes.bfloat16)
    tri = np.where(np.arange(128)[None, :] > np.arange(128)[:, None],
                   np.float32(NEG), np.float32(0.0)).astype(np.float32)

    maps = []
    for c in range(8):
        b, hh = c // 2, c % 2
        dsl = slice(2048 * hh, 2048 * hh + 2048)
        osl = slice(512 * c, 512 * c + 512)
        qw_core = np.empty((4, 128, 8 * 128), dtype=np.int32)
        for ot in range(4):
            for p in range(8):
                qw_core[ot, :, 128 * p:128 * p + 128] = \
                    qweight[p, 512 * c + 128 * ot:512 * c + 128 * ot + 128, :]
        maps.append({
            "xT": np.ascontiguousarray(x[b].T),
            "x_rows": np.ascontiguousarray(x[b, 1024 * hh:1024 * hh + 1024, :]),
            "wqT": np.ascontiguousarray(Wq[dsl, :].T),
            "wkT": np.ascontiguousarray(Wk[dsl, :].T),
            "wvT": np.ascontiguousarray(Wv[dsl, :].T),
            "woT": np.ascontiguousarray(Wo.T[dsl, :]),
            "bq_t": np.ascontiguousarray(bq[dsl].reshape(16, 128).T),
            "bk_t": np.ascontiguousarray(bk[dsl].reshape(16, 128).T),
            "bv_r": np.ascontiguousarray(bv[dsl].reshape(1, 2048)),
            "bo_r": np.ascontiguousarray(bo.reshape(1, 4096)),
            "fb_r": np.ascontiguousarray(fbias.reshape(1, 4096)),
            "qw32": qw_core,
            "lut4_t": np.ascontiguousarray(
                lut4[osl].reshape(4, 128, 16).transpose(1, 0, 2).reshape(128, 64)),
            "lut8_t": np.ascontiguousarray(
                lut8[osl].reshape(4, 128, 256).transpose(1, 0, 2).reshape(128, 1024)),
            "ident": ident,
            "ones_r": ones,
            "trineg": tri,
        })
    return maps


def kernel(**inputs):
    maps = _layout_inputs(inputs)
    nc = build_program()
    res = run_bass_kernel_spmd(nc, maps, core_ids=list(range(8)))
    out = np.empty((B, L, D), dtype=np.float32)
    for c in range(8):
        b, hh = c // 2, c % 2
        out[b, 1024 * hh:1024 * hh + 1024, :] = res.results[c]["out_rows"]
    return out


# version 04 (final)
# speedup vs baseline: 2.2271x; 1.2475x over previous
import math
import numpy as np
import ml_dtypes
from contextlib import ExitStack

import concourse.bass as bass
import concourse.mybir as mybir
from concourse.bass_utils import run_bass_kernel_spmd
from concourse.library_overlay import lower_extended_insts
from concourse.dve_op import DveOp
from concourse.dve_spec import Spec, Src0, Src1, C0, C1, eq
import concourse.dve_ops as _dve_ops

F32 = mybir.dt.float32
BF16 = mybir.dt.bfloat16
I32 = mybir.dt.int32
I16 = mybir.dt.int16
AF = mybir.ActivationFunctionType
ALU = mybir.AluOpType

B, L, D = 4, 2048, 4096
HALF = 2048
TOK = 1024
NHEAD = 16
HD = 128
OSH = 512
SCALE = 1.0 / math.sqrt(HD)
NEG = -1.0e30
Z = ("sd", 0)


def _make_eqma():
    def build(sha):
        return DveOp("EQMA", Spec(body=eq(Src0, C0) * C1 + Src1), subdim=False,
                     uops_sha=sha, perf_en={"v3": True},
                     reference=lambda in0, in1, s0, s1: (in0 == s0) * s1 + in1)
    op = build({"v3": "bcea07f103c14bf6"})
    try:
        op.compile("v3")
    except ValueError as e:
        import re as _re
        m = _re.search(r'"v3"="([0-9a-f]+)"', str(e))
        if m is None:
            raise
        op = build({"v3": m.group(1)})
    if not any(o.name == "EQMA" for o in _dve_ops.OPS):
        _dve_ops.OPS.append(op)
        _dve_ops._SUB_OPCODE_FOR_NAME["EQMA"] = (
            _dve_ops._CUSTOM_DVE_ROW_BASE + len(_dve_ops.OPS) - 1)
        _dve_ops.CUSTOM_DVE_SPECS["EQMA"] = op.spec
        _dve_ops.CUSTOM_DVE_REFS["EQMA"] = op.ref
    return op


EQMA = _make_eqma()


class Ctx:
    def __init__(self, nc, stack):
        self.nc = nc
        self.sems = {}
        for name in ("sd", "sc", "sv", "sm", "sp", "scc", "si", "sr",
                     "sxa", "sxb", "sw2", "sw3", "sw4", "sw5",
                     "se0", "se1", "svh0", "svh1", "skh0", "skh1",
                     "sqh0", "sqh1", "spo0", "spo1", "sfm0", "sfm1",
                     "sgl", "sgt", "sx1sp", "sz2", "srd", "sxr", "sag"):
            self.sems[name] = [stack.enter_context(nc.semaphore(name)), 0]

    def inc(self, name, instr, k):
        s = self.sems[name]
        instr.then_inc(s[0], k)
        s[1] += k
        return (name, s[1])

    def dma(self, eng, out, in_, sem):
        return self.inc(sem, eng.dma_start(out, in_), 16)

    def tok(self, name):
        return (name, self.sems[name][1])

    def wt(self, eng, *tokens):
        best = {}
        for t in tokens:
            if t is None:
                continue
            n, v = t
            if v > 0:
                best[n] = max(best.get(n, 0), v)
        for n, v in best.items():
            eng.wait_ge(self.sems[n][0], v)

    def wait(self, eng, *names):
        for n in names:
            s = self.sems[n]
            if s[1] > 0:
                eng.wait_ge(s[0], s[1])


def build_program(dbg=()):
    nc = bass.Bass("TRN2", target_bir_lowering=False)

    xT = nc.dram_tensor("xT", [D, L], F32, kind="ExternalInput").ap()
    x_rows = nc.dram_tensor("x_rows", [TOK, D], F32, kind="ExternalInput").ap()
    wqT = nc.dram_tensor("wqT", [D, HALF], F32, kind="ExternalInput").ap()
    wkT = nc.dram_tensor("wkT", [D, HALF], F32, kind="ExternalInput").ap()
    wvT = nc.dram_tensor("wvT", [D, HALF], F32, kind="ExternalInput").ap()
    woT = nc.dram_tensor("woT", [HALF, D], F32, kind="ExternalInput").ap()
    bq_t = nc.dram_tensor("bq_t", [128, NHEAD], F32, kind="ExternalInput").ap()
    bk_t = nc.dram_tensor("bk_t", [128, NHEAD], F32, kind="ExternalInput").ap()
    bv_r = nc.dram_tensor("bv_r", [1, HALF], F32, kind="ExternalInput").ap()
    bo_r = nc.dram_tensor("bo_r", [1, D], F32, kind="ExternalInput").ap()
    fb_r = nc.dram_tensor("fb_r", [1, D], F32, kind="ExternalInput").ap()
    qw32 = nc.dram_tensor("qw32", [4, 128, 8 * 128], I32, kind="ExternalInput").ap()
    lut4_t = nc.dram_tensor("lut4_t", [128, 4 * 16], F32, kind="ExternalInput").ap()
    lut8_t = nc.dram_tensor("lut8_t", [128, 4 * 256], F32, kind="ExternalInput").ap()
    ident = nc.dram_tensor("ident", [128, 128], BF16, kind="ExternalInput").ap()
    ones_r = nc.dram_tensor("ones_r", [1, 128], BF16, kind="ExternalInput").ap()
    trineg = nc.dram_tensor("trineg", [128, 128], F32, kind="ExternalInput").ap()

    out_rows = nc.dram_tensor("out_rows", [TOK, D], F32, kind="ExternalOutput").ap()

    qT_sp = nc.dram_tensor("qT_sp", [HALF, L], BF16, kind="Internal").ap()
    kT_sp = nc.dram_tensor("kT_sp", [HALF, L], BF16, kind="Internal").ap()
    v_sp = nc.dram_tensor("v_sp", [NHEAD, L, HD], BF16, kind="Internal").ap()
    x1_sp = nc.dram_tensor("x1_sp", [TOK, D], F32, kind="Internal").ap()
    z2_sp = nc.dram_tensor("z2_sp", [TOK, D], F32, kind="Internal").ap()
    rs_in = [nc.dram_tensor(f"rs_in{i}", [L, 1024], F32, kind="Internal").ap() for i in range(4)]
    rs_out = [nc.dram_tensor(f"rs_out{i}", [TOK, 1024], F32, kind="Internal").ap() for i in range(4)]
    ag_in = nc.dram_tensor("ag_in", [2 * D, OSH], BF16, kind="Internal").ap()
    ag_out = nc.dram_tensor("ag_out", [8 * 2 * D, OSH], BF16, kind="Internal").ap()

    dbg_t = {}
    if "x1" in dbg:
        dbg_t["d_x1"] = nc.dram_tensor("d_x1", [TOK, D], F32, kind="ExternalOutput").ap()
    if "z2" in dbg:
        dbg_t["d_z2"] = nc.dram_tensor("d_z2", [TOK, D], F32, kind="ExternalOutput").ap()
    if "w" in dbg:
        dbg_t["d_w"] = nc.dram_tensor("d_w", [2 * D, OSH], BF16, kind="ExternalOutput").ap()
    if "qkv" in dbg:
        dbg_t["d_qT"] = nc.dram_tensor("d_qT", [HALF, L], BF16, kind="ExternalOutput").ap()
        dbg_t["d_kT"] = nc.dram_tensor("d_kT", [HALF, L], BF16, kind="ExternalOutput").ap()
        dbg_t["d_v"] = nc.dram_tensor("d_v", [16, L, 128], BF16, kind="ExternalOutput").ap()
    if "ao" in dbg:
        dbg_t["d_ao"] = nc.dram_tensor("d_ao", [HALF, L], BF16, kind="ExternalOutput").ap()

    stack = ExitStack()
    with stack:
        cx = Ctx(nc, stack)
        stage = stack.enter_context(nc.sbuf_tensor("stage", [128, 4096], F32))
        ident_s = stack.enter_context(nc.sbuf_tensor("ident_s", [128, 128], BF16))
        ones_s = stack.enter_context(nc.sbuf_tensor("ones_s", [1, 128], BF16))
        trineg_s = stack.enter_context(nc.sbuf_tensor("trineg_s", [128, 128], F32))
        small = stack.enter_context(nc.sbuf_tensor("small", [128, 64], F32))
        rowbuf = stack.enter_context(nc.sbuf_tensor("rowbuf", [1, 512], F32))
        rowbf = stack.enter_context(nc.sbuf_tensor("rowbf", [1, 512], BF16))

        cx.dma(nc.sync, ident_s[:, :], ident[:, :], sem="si")
        cx.dma(nc.sync, ones_s[:, :], ones_r[:, :], sem="si")
        cx.dma(nc.sync, trineg_s[:, :], trineg[:, :], sem="si")
        cx.dma(nc.sync, small[:, 0:NHEAD], bq_t[:, :], sem="si")
        cx.dma(nc.sync, small[:, 16:16 + NHEAD], bk_t[:, :], sem="si")
        cx.inc("sv", nc.vector.memset(small[:, 48:49], 1e-5), 1)
        cx.wait(nc.tensor, "si")
        cx.wait(nc.scalar, "si")

        def bias_broadcast(row_dram, dst_sbuf, width, psum):
            for j in range(width // 512):
                cx.wait(nc.sync, "sv", "sm")
                cx.dma(nc.sync, rowbuf[0:1, 0:512], row_dram[0:1, 512 * j:512 * j + 512],
                       sem="sr")
                cx.wait(nc.vector, "sr", "sm", "si")
                cx.inc("sv", nc.vector.tensor_copy(rowbf[0:1, 0:512], rowbuf[0:1, 0:512]), 1)
                cx.wait(nc.tensor, "sv")
                cx.inc("sm", nc.tensor.matmul(psum[:, 0:512], ones_s[:, :],
                                              rowbf[0:1, 0:512], start=True, stop=True), 1)
                cx.wait(nc.vector, "sm")
                cx.inc("sv", nc.vector.tensor_copy(dst_sbuf[:, 512 * j:512 * j + 512],
                                                   psum[:, 0:512]), 1)

        with (
            nc.sbuf_tensor("qw_s", [128, 1024], I32) as qw_s,
            nc.sbuf_tensor("idxbf", [128, 4096], BF16) as idxbf,
            nc.sbuf_tensor("idx4bf", [128, 4096], BF16) as idx4bf,
            nc.sbuf_tensor("accI", [128, 4096], I32) as accI,
            nc.sbuf_tensor("tmpI", [128, 128], I32) as tmpI,
            nc.sbuf_tensor("lut4_s", [128, 64], F32) as lut4_s,
            nc.sbuf_tensor("lut8_s", [128, 256], F32) as lut8_s,
            nc.sbuf_tensor("wacc", [128, 2048], BF16) as wacc,
            nc.sbuf_tensor("ttile", [128, 512], BF16) as ttile,
            nc.psum_tensor("pt_deq", [128, 1024], BF16) as pt_deq,
        ):
            deq = _dequant_gen(cx, nc, qw32, lut4_t, lut8_t, ag_in, ag_out, ident_s,
                               qw_s, idxbf, idx4bf, accI, tmpI, lut4_s, lut8_s,
                               wacc, ttile, pt_deq, dbg_t)

            def pump(n):
                for _ in range(n):
                    if next(deq, "DONE") == "DONE":
                        return

            emit_proj(cx, nc, xT, wqT, wkT, wvT, bv_r, kT_sp, qT_sp, v_sp,
                      stage, small, bias_broadcast, pump)
            if "d_qT" in dbg_t:
                cx.wait(nc.sync, "se0", "se1", "sv", "sc")
                cx.dma(nc.sync, dbg_t["d_qT"][:, :], qT_sp[:, :], sem="sp")
                cx.dma(nc.sync, dbg_t["d_kT"][:, :], kT_sp[:, :], sem="sp")
                cx.dma(nc.sync, dbg_t["d_v"][:, :, :], v_sp[:, :, :], sem="sp")
            with nc.sbuf_tensor("aoT", [128, 16 * L], BF16) as aoT:
                emit_attn(cx, nc, kT_sp, aoT, qT_sp, v_sp, ident_s, trineg_s, pump)
                if "d_ao" in dbg_t:
                    cx.wait(nc.sync, "sv", "sm")
                    for k in range(16):
                        cx.dma(nc.sync, dbg_t["d_ao"][128 * k:128 * k + 128, :],
                               aoT[:, L * k:L * k + L], sem="sp")
                    cx.wait(nc.sync, "sp")
                emit_wo_rs(cx, nc, aoT, woT, stage, rs_in, rs_out, pump)
            pump(10 ** 9)

        emit_ln1_ffn_ln2(cx, nc, x_rows, x1_sp, z2_sp, rs_out, ag_out, bo_r, fb_r,
                         out_rows, stage, small, ident_s, bias_broadcast, dbg_t)
        cx.wait(nc.sync, "sp")
    lower_extended_insts(nc)
    return nc


def _dequant_gen(cx, nc, qw32, lut4_t, lut8_t, ag_in, ag_out, ident_s,
                 qw_s, idxbf, idx4bf, accI, tmpI, lut4_s, lut8_s,
                 wacc, ttile, pt_deq, dbg_t):
    cx.dma(nc.sync, lut4_s[:, :], lut4_t[:, :], sem="si")
    jl_tok = Z
    tp_tok = Z
    ttile_dma = Z

    ev_prev = Z

    def jloop(nj, lut_s, base, dst0, idx_ap, i0, ot):
        nonlocal jl_tok, tp_tok, ttile_dma, ev_prev
        cx.wt(nc.vector, tp_tok)
        for j in range(nj):
            col = lut_s[:, base + j:base + j + 1]
            if j == 0:
                jl_tok = cx.inc("sv", nc.vector.tensor_scalar(
                    wacc[:, :], idx_ap, 0, col, ALU.is_equal, ALU.mult), 1)
            else:
                jl_tok = cx.inc("sv", nc.vector._custom_dve(
                    EQMA, out=wacc[:, :], in0=idx_ap, in1=wacc[:, :],
                    s0=float(j), s1=col), 1)
            if j % 16 == 15:
                yield
        for g in range(4):
            cx.wt(nc.tensor, jl_tok, ttile_dma, ev_prev)
            for k in range(4):
                kk = 4 * g + k
                tp_tok = cx.inc("sm", nc.tensor.transpose(
                    pt_deq[:, 128 * k:128 * k + 128],
                    wacc[:, 128 * kk:128 * kk + 128], ident_s[:, :]), 1)
            cx.wt(nc.vector, tp_tok, ttile_dma)
            ev = cx.inc("sv", nc.vector.tensor_copy(ttile[:, :], pt_deq[:, 0:512]), 1)
            ev_prev = ev
            cx.wt(nc.gpsimd, ev)
            for k in range(4):
                kk = 4 * g + k
                ttile_dma = cx.dma(
                    nc.gpsimd,
                    ag_in[dst0 + i0 + 128 * kk: dst0 + i0 + 128 * kk + 128,
                          128 * ot:128 * ot + 128],
                    ttile[:, 128 * k:128 * k + 128], sem="sgt")
            if g % 2 == 1:
                yield

    for ot in range(4):
        cx.wt(nc.gpsimd, jl_tok)
        qw_tok = cx.dma(nc.gpsimd, qw_s[:, :], qw32[ot, :, :], sem="sgl")
        cx.wt(nc.gpsimd, jl_tok)
        l8tok = cx.dma(nc.gpsimd, lut8_s[:, :], lut8_t[:, 256 * ot:256 * ot + 256], sem="sgl")
        cx.wt(nc.vector, qw_tok, l8tok, cx.tok("si") if ot == 0 else None)
        acc3 = accI.ap().rearrange("p (w x) -> p w x", x=32)
        for s in range(32):
            for p in range(8):
                if p == 0:
                    jl_tok = cx.inc("sv", nc.vector.tensor_scalar(
                        acc3[:, :, s], qw_s[:, 0:128], int(s), 1,
                        ALU.logical_shift_right, ALU.bitwise_and), 1)
                else:
                    srcp = qw_s[:, 128 * p:128 * p + 128]
                    if s >= p:
                        jl_tok = cx.inc("sv", nc.vector.tensor_scalar(
                            tmpI[:, :], srcp, int(s - p), int(1 << p),
                            ALU.logical_shift_right, ALU.bitwise_and), 1)
                    else:
                        jl_tok = cx.inc("sv", nc.vector.tensor_scalar(
                            tmpI[:, :], srcp, int(p - s), int(1 << p),
                            ALU.logical_shift_left, ALU.bitwise_and), 1)
                    jl_tok = cx.inc("sv", nc.vector.tensor_tensor(
                        acc3[:, :, s], acc3[:, :, s], tmpI[:, :], ALU.bitwise_or), 1)
            if s % 8 == 7:
                yield
        jl_tok = cx.inc("sv", nc.vector.tensor_copy(idxbf[:, :], accI[:, :]), 1)
        jl_tok = cx.inc("sv", nc.vector.tensor_single_scalar(
            accI[:, :], accI[:, :], 15, ALU.bitwise_and), 1)
        jl_tok = cx.inc("sv", nc.vector.tensor_copy(idx4bf[:, :], accI[:, :]), 1)
        yield
        for ih in range(2):
            i0 = 2048 * ih
            yield from jloop(16, lut4_s, 16 * ot, 0, idx4bf[:, i0:i0 + 2048], i0, ot)
            yield from jloop(256, lut8_s, 0, D, idxbf[:, i0:i0 + 2048], i0, ot)
    cx.wt(nc.gpsimd, ttile_dma)
    cx.inc("sag", nc.gpsimd.collective_compute(
        "AllGather", ALU.bypass,
        replica_groups=[[0, 1, 2, 3, 4, 5, 6, 7]],
        ins=[ag_in[:, :]], outs=[ag_out[:, :]]), 1)
    if "d_w" in dbg_t:
        cx.wait(nc.sync, "sgt")
        cx.dma(nc.sync, dbg_t["d_w"][:, :], ag_in[:, :], sem="sp")


def emit_proj(cx, nc, xT, wqT, wkT, wvT, bv_r, kT_sp, qT_sp, v_sp,
              stage, small, bias_broadcast, pump):
    with (
        nc.sbuf_tensor("xt_h", [128, 32 * 1024], BF16) as xt_h,
        nc.sbuf_tensor("wbuf0", [128, 32 * 128], BF16) as wbuf0,
        nc.sbuf_tensor("wbuf1", [128, 32 * 128], BF16) as wbuf1,
        nc.sbuf_tensor("vbias", [128, HALF], BF16) as vbias,
        nc.sbuf_tensor("evq0", [128, 512], BF16) as evq0,
        nc.sbuf_tensor("evq1", [128, 512], BF16) as evq1,
        nc.psum_tensor("pp0", [128, 512], F32) as pp0,
        nc.psum_tensor("pp1", [128, 512], F32) as pp1,
    ):
        bias_broadcast(bv_r[:, :], vbias, HALF, pp0)
        vbias_tok = cx.tok("sv")
        wbufs = [wbuf0, wbuf1]
        wfree = [Z, Z]
        wready = [Z, Z]
        psums = [pp0, pp1]
        pfree = [Z, Z]
        evqs = [evq0, evq1]
        efree = [Z, Z]
        stg = [Z] * 6

        def load_w_16th(wsrc, os_, buf):
            o0 = 128 * os_
            ctok = Z
            for g in range(8):
                sl = 2 + g % 2
                c0 = 1024 + 512 * (g % 2)
                cx.wt(nc.sync, stg[sl], wfree[buf] if g == 0 else None)
                dtok = cx.dma(
                    nc.sync,
                    stage[:, c0:c0 + 512].rearrange("p (a o) -> p a o", o=128),
                    wsrc[512 * g:512 * g + 512, o0:o0 + 128].rearrange(
                        "(a p) o -> p a o", p=128),
                    sem=f"sw{sl}")
                cx.wt(nc.scalar, dtok, wfree[buf] if g == 0 else None)
                ctok = cx.inc("sc", nc.scalar.copy(
                    wbufs[buf][:, 512 * g:512 * g + 512], stage[:, c0:c0 + 512]), 1)
                stg[sl] = ctok
            wready[buf] = ctok

        for th in range(2):
            t0 = 1024 * th
            xt_last = Z
            for k in range(32):
                sl = k % 2
                xsem = "sxa" if sl == 0 else "sxb"
                cx.wt(nc.sync, stg[sl])
                cx.dma(nc.sync, stage[:, 512 * sl:512 * sl + 512],
                       xT[128 * k:128 * k + 128, t0:t0 + 512], sem=xsem)
                dtok2 = cx.dma(nc.sync, stage[:, 512 * sl + 2048:512 * sl + 2048 + 512],
                               xT[128 * k:128 * k + 128, t0 + 512:t0 + 1024], sem=xsem)
                cx.wt(nc.scalar, dtok2, wfree[0] if k == 0 else None,
                      wfree[1] if k == 0 else None)
                cx.inc("sc", nc.scalar.copy(
                    xt_h[:, 1024 * k:1024 * k + 512], stage[:, 512 * sl:512 * sl + 512]), 1)
                ctok = cx.inc("sc", nc.scalar.copy(
                    xt_h[:, 1024 * k + 512:1024 * k + 1024],
                    stage[:, 512 * sl + 2048:512 * sl + 2048 + 512]), 1)
                stg[sl] = ctok
                xt_last = ctok
            pump(4)
            for (wsrc, kind) in ((wkT, "k"), (wvT, "v"), (wqT, "q")):
                load_w_16th(wsrc, 0, 0)
                for os_ in range(16):
                    buf = os_ % 2
                    if os_ < 15:
                        load_w_16th(wsrc, os_ + 1, (os_ + 1) % 2)
                    g = 0
                    og = os_
                    if kind in ("q", "k"):
                        for tb in range(2):
                            pb = psums[g % 2]
                            cx.wt(nc.tensor, wready[buf], xt_last, pfree[g % 2])
                            mt = Z
                            for k in range(32):
                                mt = cx.inc("sm", nc.tensor.matmul(
                                    pb[:, :],
                                    wbufs[buf][:, 128 * k:128 * k + 128],
                                    xt_h[:, 1024 * k + 512 * tb: 1024 * k + 512 * tb + 512],
                                    start=(k == 0), stop=(k == 31)), 1)
                            wfree[buf] = mt
                            bcol = (0 if kind == "q" else 16) + og
                            spt = qT_sp if kind == "q" else kT_sp
                            cx.wt(nc.scalar, mt, efree[g % 2])
                            ev = cx.inc("sc", nc.scalar.add(
                                evqs[g % 2][:, :], pb[:, :], small[:, bcol:bcol + 1]), 1)
                            cx.wt(nc.sync, ev)
                            efree[g % 2] = cx.dma(
                                nc.sync, spt[128 * og:128 * og + 128,
                                             t0 + 512 * tb: t0 + 512 * tb + 512],
                                evqs[g % 2][:, :], sem=f"se{g % 2}")
                            pfree[g % 2] = ev
                            g += 1
                        if os_ % 2 == 1:
                            pump(2 if kind == "k" else 1)
                    else:
                        for tt in range(8):
                            vb = tt // 4
                            col = tt % 4
                            pb = psums[g % 2]
                            cx.wt(nc.tensor, wready[buf], xt_last, pfree[g % 2])
                            mt = Z
                            for k in range(32):
                                mt = cx.inc("sm", nc.tensor.matmul(
                                    pb[:, 0:128],
                                    xt_h[:, 1024 * k + 128 * tt: 1024 * k + 128 * tt + 128],
                                    wbufs[buf][:, 128 * k:128 * k + 128],
                                    start=(k == 0), stop=(k == 31)), 1)
                            wfree[buf] = mt
                            cx.wt(nc.vector, mt, vbias_tok,
                                  efree[vb % 2] if col == 0 else None)
                            ev = cx.inc("sv", nc.vector.tensor_add(
                                evqs[vb % 2][:, 128 * col:128 * col + 128], pb[:, 0:128],
                                vbias[:, 128 * os_:128 * os_ + 128]), 1)
                            pfree[g % 2] = ev
                            g += 1
                            if col == 3:
                                cx.wt(nc.sync, ev)
                                efree[vb % 2] = cx.dma(
                                    nc.sync,
                                    v_sp[os_, t0 + 512 * vb: t0 + 512 * vb + 512, :]
                                    .rearrange("(a p) d -> p a d", p=128),
                                    evqs[vb % 2][:, :].rearrange("p (a d) -> p a d", d=128),
                                    sem=f"se{vb % 2}")
                        if tt == 7 and os_ % 2 == 1:
                            pump(2)


def emit_attn(cx, nc, kT_sp, aoT, qT_sp, v_sp, ident_s, trineg_s, pump):
    kq_tok = cx.tok("sc")
    sp_q0, sp_q1 = cx.tok("se0"), cx.tok("se1")
    with (
        nc.sbuf_tensor("kh0", [128, L], BF16) as kh0,
        nc.sbuf_tensor("kh1", [128, L], BF16) as kh1,
        nc.sbuf_tensor("vh0", [128, 16 * 128], BF16) as vh0,
        nc.sbuf_tensor("vh1", [128, 16 * 128], BF16) as vh1,
        nc.sbuf_tensor("qh0", [128, 512], BF16) as qh0,
        nc.sbuf_tensor("qh1", [128, 512], BF16) as qh1,
        nc.sbuf_tensor("ebf0", [128, 2048], BF16) as ebf0,
        nc.sbuf_tensor("ebf1", [128, 2048], BF16) as ebf1,
        nc.sbuf_tensor("etb", [128, 16 * 512], BF16) as etb,
        nc.sbuf_tensor("zrow", [128, 16], F32) as zrow,
        nc.psum_tensor("ps_s", [128, 4 * 512], F32) as ps_s,
        nc.psum_tensor("ps_t", [128, 2048], BF16) as ps_t,
        nc.psum_tensor("ps_av", [128, 512], F32) as ps_av,
    ):
        vhs, qhs, khs = [vh0, vh1], [qh0, qh1], [kh0, kh1]
        ebfs = [ebf0, ebf1]
        vh_tok = [Z, Z]
        vh_free = [Z, Z]
        kh_tok = [Z, Z]
        kh_free = [Z, Z]
        qh_tok = [Z, Z]
        qh_free = [Z, Z]
        exp_col = [Z, Z, Z, Z]
        tp_free = [Z, Z]
        et_ev = Z
        et_ms = Z
        av_tok = Z
        ao_ev = Z
        ptg_free = [Z, Z]
        it = 0
        prev = None
        vh3 = [v.ap().rearrange("p (k d) -> p k d", d=128) for v in vhs]

        def stage_b(p_):
            nonlocal et_ev, et_ms
            (pib, pext, pnch, pntk, pti, psb, pexp) = p_
            cx.wt(nc.vector, pexp)
            red = cx.inc("sv", nc.vector.reduce_sum(
                zrow[:, 8 + pib:9 + pib], zrow[:, 4 * pib:4 * pib + pnch],
                axis=mybir.AxisListType.X), 1)
            cx.wt(nc.scalar, red)
            lnt = cx.inc("sc", nc.scalar.activation(
                zrow[:, 10 + pib:11 + pib], zrow[:, 8 + pib:9 + pib], AF.Ln), 1)
            cx.wt(nc.vector, lnt)
            neg = cx.inc("sv", nc.vector.tensor_scalar(
                zrow[:, 12 + pib:13 + pib], zrow[:, 10 + pib:11 + pib],
                -1.0, None, ALU.mult), 1)
            cx.wt(nc.scalar, neg)
            inv = cx.inc("sc", nc.scalar.activation(
                zrow[:, 14 + pib:15 + pib], zrow[:, 12 + pib:13 + pib], AF.Exp), 1)
            cx.wt(nc.vector, inv)
            ebm = cx.inc("sv", nc.vector.tensor_scalar(
                ebfs[pib][:, 0:pext], ebfs[pib][:, 0:pext],
                zrow[:, 14 + pib:15 + pib], None, ALU.mult), 1)
            cx.wt(nc.vector, av_tok)
            for k in range(pntk, 4 * (psb + 1)):
                et_ms = cx.inc("sv", nc.vector.memset(
                    etb[:, 512 * k + 128 * pti: 512 * k + 128 * pti + 128], 0.0), 1)
            for g in range((pntk + 3) // 4):
                gb = g % 2
                k0 = 4 * g
                kn = min(4, pntk - k0)
                cx.wt(nc.tensor, ebm, ptg_free[gb])
                tt_ = Z
                for k in range(kn):
                    tt_ = cx.inc("sm", nc.tensor.transpose(
                        ps_t[:, 1024 * gb + 128 * k:1024 * gb + 128 * k + 128],
                        ebfs[pib][:, 128 * (k0 + k):128 * (k0 + k) + 128], ident_s[:, :]), 1)
                tp_free[pib] = tt_
                cx.wt(nc.scalar, tt_, av_tok)
                dst = etb.ap().rearrange("p (k t) -> p k t", t=512)
                ev = cx.inc("sc", nc.scalar.copy(
                    dst[:, k0:k0 + kn, 128 * pti:128 * pti + 128],
                    ps_t.ap().rearrange("p (k t) -> p k t", t=128)[:, 8 * gb:8 * gb + kn, :]), 1)
                ptg_free[gb] = ev
                et_ev = ev
        cx.wt(nc.sync, kq_tok, sp_q0, sp_q1)
        vh_tok[0] = cx.dma(nc.sync, vh3[0][:, :, :],
                           v_sp[0, :, :].rearrange("(k p) d -> p k d", p=128), sem="svh0")
        kh_tok[0] = cx.dma(nc.sync, khs[0][:, :], kT_sp[0:128, :], sem="skh0")
        for h in range(NHEAD):
            hb = h % 2
            if h + 1 < NHEAD:
                cx.wt(nc.sync, vh_free[(h + 1) % 2])
                vh_tok[(h + 1) % 2] = cx.dma(nc.sync, vh3[(h + 1) % 2][:, :, :],
                                             v_sp[h + 1, :, :].rearrange("(k p) d -> p k d", p=128),
                                             sem=f"svh{(h + 1) % 2}")
                cx.wt(nc.sync, kh_free[(h + 1) % 2])
                kh_tok[(h + 1) % 2] = cx.dma(nc.sync, khs[(h + 1) % 2][:, :],
                                             kT_sp[128 * (h + 1):128 * (h + 1) + 128, :],
                                             sem=f"skh{(h + 1) % 2}")
            for sb in range(4):
                qb = sb % 2
                cx.wt(nc.sync, qh_free[qb])
                qh_tok[qb] = cx.dma(nc.sync, qhs[qb][:, :],
                                    qT_sp[128 * h:128 * h + 128, 512 * sb:512 * sb + 512],
                                    sem=f"sqh{qb}")
                for ti in range(4):
                    ib = it % 2
                    it += 1
                    tq0 = 512 * sb + 128 * ti
                    ext = tq0 + 128
                    nch = (ext + 511) // 512
                    ntk = (ext + 127) // 128
                    mm_ch = [Z, Z, Z, Z]
                    for tkb in range(sb + 1):
                        n = 512 if tkb < sb else 128 * (ti + 1)
                        cx.wt(nc.tensor, exp_col[tkb], qh_tok[qb], kh_tok[hb])
                        mm_ch[tkb] = cx.inc("sm", nc.tensor.matmul(
                            ps_s[:, 512 * tkb:512 * tkb + n],
                            qhs[qb][:, 128 * ti:128 * ti + 128],
                            khs[hb][:, 512 * tkb: 512 * tkb + n],
                            start=True, stop=True), 1)
                    qh_free[qb] = mm_ch[sb]
                    kh_free[hb] = mm_ch[sb]
                    cx.wt(nc.vector, mm_ch[sb])
                    msk = cx.inc("sv", nc.vector.tensor_add(
                        ps_s[:, ext - 128:ext], ps_s[:, ext - 128:ext], trineg_s[:, :]), 1)
                    exp_last = Z
                    for ch in range(nch):
                        e0 = 512 * ch
                        e1 = min(ext, e0 + 512)
                        cx.wt(nc.scalar, msk if ch == nch - 1 else mm_ch[ch],
                              tp_free[ib] if ch == 0 else None)
                        exp_last = cx.inc("sc", nc.scalar.activation(
                            ebfs[ib][:, e0:e1], ps_s[:, e0:e1], AF.Exp, scale=SCALE,
                            accum_out=zrow[:, 4 * ib + ch:4 * ib + ch + 1]), 1)
                        exp_col[ch] = exp_last
                    cur = (ib, ext, nch, ntk, ti, sb, exp_last)
                    if prev is not None:
                        stage_b(prev)
                    prev = cur
                    if it % 6 == 0:
                        pump(1)
                if prev is not None:
                    stage_b(prev)
                    prev = None
                cx.wt(nc.tensor, et_ev, et_ms, vh_tok[hb], ao_ev)
                ntk_sb = 4 * (sb + 1)
                for k in range(ntk_sb):
                    av_tok = cx.inc("sm", nc.tensor.matmul(
                        ps_av[:, :], vhs[hb][:, 128 * k:128 * k + 128],
                        etb[:, 512 * k:512 * k + 512],
                        start=(k == 0), stop=(k == ntk_sb - 1)), 1)
                vh_free[hb] = av_tok
                cx.wt(nc.vector, av_tok)
                ao_ev = cx.inc("sv", nc.vector.tensor_copy(
                    aoT[:, L * h + 512 * sb: L * h + 512 * sb + 512], ps_av[:, :]), 1)
            pump(1)


def emit_wo_rs(cx, nc, aoT, woT, stage, rs_in, rs_out, pump):
    ao_done = cx.tok("sv")
    with (
        nc.sbuf_tensor("wo0", [128, 16 * 512], BF16) as wo0,
        nc.sbuf_tensor("wo1", [128, 16 * 512], BF16) as wo1,
        nc.sbuf_tensor("poev0", [128, 512], F32) as poev0,
        nc.sbuf_tensor("poev1", [128, 512], F32) as poev1,
        nc.psum_tensor("pw0", [128, 512], F32) as pw0,
        nc.psum_tensor("pw1", [128, 512], F32) as pw1,
    ):
        wos = [wo0, wo1]
        wready = [Z, Z]
        wfree = [Z, Z]
        poevs = [poev0, poev1]
        pofree = [Z, Z]
        pfree = [Z, Z]
        psums = [pw0, pw1]
        stg = [cx.tok("sc")] * 4

        def load_wo(ob, buf):
            ctok = Z
            o0 = 512 * ob
            for g4 in range(4):
                sl = g4 % 2
                cx.wt(nc.sync, stg[sl], wfree[buf] if g4 == 0 else None)
                dtok = cx.dma(
                    nc.sync,
                    stage[:, 2048 * sl:2048 * sl + 2048].rearrange("p (a o) -> p a o", o=512),
                    woT[512 * g4:512 * g4 + 512, o0:o0 + 512].rearrange(
                        "(a p) o -> p a o", p=128),
                    sem=f"sw{2 + sl}")
                cx.wt(nc.scalar, dtok, wfree[buf] if g4 == 0 else None)
                ctok = cx.inc("sc", nc.scalar.copy(
                    wos[buf][:, 2048 * g4:2048 * g4 + 2048],
                    stage[:, 2048 * sl:2048 * sl + 2048]), 1)
                stg[sl] = ctok
            wready[buf] = ctok

        load_wo(0, 0)
        for ob in range(8):
            buf = ob % 2
            if ob < 7:
                load_wo(ob + 1, (ob + 1) % 2)
            for tt in range(16):
                g = tt % 2
                pb = psums[g]
                cx.wt(nc.tensor, wready[buf], ao_done, pfree[g])
                mt = Z
                for k in range(16):
                    mt = cx.inc("sm", nc.tensor.matmul(
                        pb[:, :],
                        aoT[:, L * k + 128 * tt: L * k + 128 * tt + 128],
                        wos[buf][:, 512 * k:512 * k + 512],
                        start=(k == 0), stop=(k == 15)), 1)
                wfree[buf] = mt
                cx.wt(nc.vector, mt, pofree[g])
                ev = cx.inc("sv", nc.vector.tensor_copy(poevs[g][:, :], pb[:, :]), 1)
                pfree[g] = ev
                cx.wt(nc.sync, ev)
                pofree[g] = cx.dma(nc.sync, rs_in[ob // 2][128 * tt:128 * tt + 128,
                                                          512 * (ob % 2):512 * (ob % 2) + 512],
                                   poevs[g][:, :], sem=f"spo{g}")
            pump(5)
            if ob % 2 == 1:
                c4 = ob // 2
                cx.wt(nc.gpsimd, pofree[0], pofree[1])
                cx.inc("scc", nc.gpsimd.collective_compute(
                    "ReduceScatter", ALU.add,
                    replica_groups=[[0, 1], [2, 3], [4, 5], [6, 7]],
                    ins=[rs_in[c4][:, :]], outs=[rs_out[c4][:, :]]), 1)


def emit_ln1_ffn_ln2(cx, nc, x_rows, x1_sp, z2_sp, rs_out, ag_out, bo_r, fb_r,
                     out_rows, stage, small, ident_s, bias_broadcast, dbg_t):
    with (
        nc.sbuf_tensor("x1T", [128, 32 * 1024], BF16) as x1T,
        nc.sbuf_tensor("bias_bc", [128, D], F32) as bias_bc,
    ):
        with (
            nc.sbuf_tensor("zn", [128, 4096], F32) as zn,
            nc.sbuf_tensor("x1bf", [128, 4096], BF16) as x1bf,
            nc.sbuf_tensor("red", [128, 32], F32) as red,
            nc.psum_tensor("pb0", [128, 512], F32) as pb0,
            nc.psum_tensor("pt4", [128, 2048], BF16) as pt4,
        ):
            bias_broadcast(bo_r[:, :], bias_bc, D, pb0)
            cx.wait(nc.sync, "scc")
            ptg_free = [Z, Z]
            for tt in range(8):
                cx.wait(nc.sync, "sv", "sc", "sm", "sx1sp")
                for c4 in range(4):
                    cx.dma(nc.sync, zn[:, 1024 * c4:1024 * c4 + 1024],
                           rs_out[c4][128 * tt:128 * tt + 128, :], sem="srd")
                cx.dma(nc.sync, stage[:, 0:4096], x_rows[128 * tt:128 * tt + 128, :], sem="srd")
                cx.wait(nc.vector, "srd", "sc", "sm", "sx1sp")
                cx.inc("sv", nc.vector.tensor_add(zn[:, :], zn[:, :], stage[:, 0:4096]), 1)
                cx.inc("sv", nc.vector.tensor_add(zn[:, :], zn[:, :], bias_bc[:, :]), 1)
                _ln_stats(cx, nc, zn, red, stage, small[:, 48:49])
                cx.wait(nc.vector, "sc")
                cx.inc("sv", nc.vector.tensor_scalar(zn[:, :], zn[:, :], red[:, 26:27], None, ALU.mult), 1)
                cx.wait(nc.sync, "sv")
                cx.dma(nc.sync, x1_sp[128 * tt:128 * tt + 128, :], zn[:, :], sem="sx1sp")
                cx.wait(nc.scalar, "sv")
                for ch in range(8):
                    cx.inc("sc", nc.scalar.activation(stage[:, 512 * ch:512 * ch + 512],
                                                      zn[:, 512 * ch:512 * ch + 512], AF.Square,
                                                      accum_out=red[:, 16 + ch:17 + ch]), 1)
                cx.wait(nc.vector, "sc")
                cx.inc("sv", nc.vector.reduce_sum(red[:, 30:31], red[:, 16:24],
                                                  axis=mybir.AxisListType.X), 1)
                cx.wait(nc.scalar, "sv")
                cx.inc("sc", nc.scalar.mul(red[:, 31:32], red[:, 30:31], 1.0), 1)
                cx.wait(nc.vector, "sc")
                cx.inc("sv", nc.vector.tensor_scalar(small[:, 32 + tt:33 + tt], red[:, 31:32],
                                                     float(D), None, ALU.is_gt), 1)
                cv = cx.inc("sv", nc.vector.tensor_copy(x1bf[:, :], zn[:, :]), 1)
                for g in range(8):
                    gb = g % 2
                    cx.wt(nc.tensor, cv, ptg_free[gb])
                    tt_ = Z
                    for k in range(4):
                        tt_ = cx.inc("sm", nc.tensor.transpose(
                            pt4[:, 1024 * gb + 128 * k:1024 * gb + 128 * k + 128],
                            x1bf[:, 128 * (4 * g + k):128 * (4 * g + k) + 128], ident_s[:, :]), 1)
                    cx.wt(nc.vector, tt_)
                    dst = x1T.ap().rearrange("p (k t) -> p k t", t=1024)
                    ev = cx.inc("sv", nc.vector.tensor_copy(
                        dst[:, 4 * g:4 * g + 4, 128 * tt:128 * tt + 128],
                        pt4.ap().rearrange("p (k t) -> p k t", t=128)[:, 8 * gb:8 * gb + 4, :]), 1)
                    ptg_free[gb] = ev
            if "d_x1" in dbg_t:
                cx.wait(nc.sync, "sx1sp", "sv")
                cx.dma(nc.sync, dbg_t["d_x1"][:, :], x1_sp[:, :], sem="sp")

        with (
            nc.sbuf_tensor("fm0", [128, 32 * 512], BF16) as fm0,
            nc.sbuf_tensor("fm1", [128, 32 * 512], BF16) as fm1,
            nc.sbuf_tensor("y4b", [128, 4 * 512], F32) as y4b,
            nc.sbuf_tensor("ysel", [128, 512], F32) as ysel,
            nc.psum_tensor("pf0", [128, 512], F32) as pf0,
            nc.psum_tensor("pf1", [128, 512], F32) as pf1,
        ):
            bias_broadcast(fb_r[:, :], bias_bc, D, pf0)
            cx.wait(nc.sync, "sag", "sc", "sx1sp")
            x1T_tok = cx.tok("sv")
            fms = [fm0, fm1]
            fready = [Z, Z]
            ffree = [Z, Z]
            psums = [pf0, pf1]
            pfree = [Z, Z]
            y4_tok = [Z, Z, Z, Z]
            sel_done = Z
            x1c_tok = Z

            def load_fm(quad, ob, mat, buf):
                base = 8192 * ob + 4096 * mat
                cx.wt(nc.sync, ffree[buf])
                dt = Z
                for j in range(8):
                    dt = cx.dma(nc.sync,
                                fms[buf][:, 2048 * j:2048 * j + 2048].rearrange(
                                    "p (a o) -> p a o", o=512),
                                ag_out[base + 512 * j:base + 512 * j + 512, :].rearrange(
                                    "(a p) o -> p a o", p=128), sem=f"sfm{buf}")
                fready[buf] = dt

            seq = [(q, ob, mat) for q in range(2) for ob in range(8) for mat in range(2)]
            load_fm(*seq[0], 0)
            for i, (quad, ob, mat) in enumerate(seq):
                buf = i % 2
                if i + 1 < len(seq):
                    load_fm(*seq[i + 1], (i + 1) % 2)
                for ti in range(4):
                    tt = 4 * quad + ti
                    g = ti % 2
                    pb = psums[g]
                    cx.wt(nc.tensor, fready[buf], x1T_tok, pfree[g])
                    mt = Z
                    for k in range(32):
                        mt = cx.inc("sm", nc.tensor.matmul(
                            pb[:, :],
                            x1T[:, 1024 * k + 128 * tt: 1024 * k + 128 * tt + 128],
                            fms[buf][:, 512 * k:512 * k + 512],
                            start=(k == 0), stop=(k == 31)), 1)
                    ffree[buf] = mt
                    if mat == 0:
                        cx.wt(nc.vector, mt, y4_tok[ti])
                        ev = cx.inc("sv", nc.vector.tensor_copy(
                            y4b[:, 512 * ti:512 * ti + 512], pb[:, :]), 1)
                        pfree[g] = ev
                    else:
                        cx.wt(nc.sync, x1c_tok)
                        xd = cx.dma(nc.sync, stage[:, 0:512],
                                    x1_sp[128 * tt:128 * tt + 128, 512 * ob:512 * ob + 512],
                                    sem="sxr")
                        cx.wt(nc.vector, mt, sel_done)
                        s1 = cx.inc("sv", nc.vector.tensor_sub(
                            ysel[:, :], pb[:, :], y4b[:, 512 * ti:512 * ti + 512]), 1)
                        pfree[g] = s1
                        s2 = cx.inc("sv", nc.vector.scalar_tensor_tensor(
                            ysel[:, :], ysel[:, :], small[:, 32 + tt:33 + tt],
                            y4b[:, 512 * ti:512 * ti + 512], ALU.mult, ALU.add), 1)
                        y4_tok[ti] = s2
                        cx.wt(nc.vector, xd)
                        x1c_tok = cx.inc("sv", nc.vector.tensor_add(ysel[:, :], ysel[:, :],
                                                                    stage[:, 0:512]), 1)
                        s4 = cx.inc("sv", nc.vector.tensor_add(
                            ysel[:, :], ysel[:, :], bias_bc[:, 512 * ob:512 * ob + 512]), 1)
                        cx.wt(nc.sync, s4)
                        sel_done = cx.dma(nc.sync, z2_sp[128 * tt:128 * tt + 128,
                                                         512 * ob:512 * ob + 512],
                                          ysel[:, :], sem="sz2")
            if "d_z2" in dbg_t:
                cx.wait(nc.sync, "sz2")
                cx.dma(nc.sync, dbg_t["d_z2"][:, :], z2_sp[:, :], sem="sp")

        with (
            nc.sbuf_tensor("zn2", [128, 4096], F32) as zn2,
            nc.sbuf_tensor("red2", [128, 32], F32) as red2,
        ):
            cx.wait(nc.sync, "sz2")
            for tt in range(8):
                cx.wait(nc.sync, "sv", "sc", "sp")
                cx.dma(nc.sync, zn2[:, :], z2_sp[128 * tt:128 * tt + 128, :], sem="srd")
                cx.wait(nc.vector, "srd", "sc")
                _ln_stats(cx, nc, zn2, red2, stage, small[:, 48:49])
                cx.wait(nc.vector, "sc")
                cx.inc("sv", nc.vector.tensor_scalar(zn2[:, :], zn2[:, :], red2[:, 26:27],
                                                     None, ALU.mult), 1)
                cx.wait(nc.sync, "sv")
                cx.dma(nc.sync, out_rows[128 * tt:128 * tt + 128, :], zn2[:, :], sem="sp")


def _ln_stats(cx, nc, zn, red, scratch, eps_ap):
    cx.inc("sv", nc.vector.reduce_sum(red[:, 0:1], zn[:, :],
                                      axis=mybir.AxisListType.X), 1)
    cx.wait(nc.scalar, "sv")
    cx.inc("sc", nc.scalar.mul(red[:, 9:10], red[:, 0:1], -1.0 / D), 1)
    cx.wait(nc.vector, "sc")
    cx.inc("sv", nc.vector.tensor_copy(red[:, 10:11], red[:, 9:10]), 1)
    cx.wait(nc.scalar, "sv")
    cx.inc("sc", nc.scalar.add(zn[:, :], zn[:, :], red[:, 10:11]), 1)
    for ch in range(8):
        cx.inc("sc", nc.scalar.activation(scratch[:, 512 * ch:512 * ch + 512],
                                          zn[:, 512 * ch:512 * ch + 512], AF.Square,
                                          accum_out=red[:, 16 + ch:17 + ch]), 1)
    cx.wait(nc.vector, "sc")
    cx.inc("sv", nc.vector.reduce_sum(red[:, 24:25], red[:, 16:24],
                                      axis=mybir.AxisListType.X), 1)
    cx.wait(nc.scalar, "sv")
    cx.inc("sc", nc.scalar.activation(red[:, 25:26], red[:, 24:25], AF.Ln,
                                      scale=1.0 / D, bias=eps_ap), 1)
    cx.wait(nc.vector, "sc")
    cx.inc("sv", nc.vector.tensor_scalar(red[:, 27:28], red[:, 25:26], -0.5, None, ALU.mult), 1)
    cx.wait(nc.scalar, "sv")
    cx.inc("sc", nc.scalar.activation(red[:, 26:27], red[:, 27:28], AF.Exp), 1)


def _layout_inputs(ins):
    x = np.ascontiguousarray(ins["L0_attn_x"])
    Wq, Wk, Wv, Wo = (np.asarray(ins["L0_attn_Wq"]), np.asarray(ins["L0_attn_Wk"]),
                      np.asarray(ins["L0_attn_Wv"]), np.asarray(ins["L0_attn_Wo"]))
    bq, bk, bv, bo = (np.asarray(ins["L0_attn_bq"]), np.asarray(ins["L0_attn_bk"]),
                      np.asarray(ins["L0_attn_bv"]), np.asarray(ins["L0_attn_bo"]))
    qweight = np.asarray(ins["L0_ffn_qweight"])
    lut4 = np.asarray(ins["L0_ffn_lut4"])
    lut8 = np.asarray(ins["L0_ffn_lut8"])
    fbias = np.asarray(ins["L0_ffn_bias"])

    ident = np.eye(128, dtype=ml_dtypes.bfloat16)
    ones = np.ones((1, 128), dtype=ml_dtypes.bfloat16)
    tri = np.where(np.arange(128)[None, :] > np.arange(128)[:, None],
                   np.float32(NEG), np.float32(0.0)).astype(np.float32)

    maps = []
    for c in range(8):
        b, hh = c // 2, c % 2
        dsl = slice(2048 * hh, 2048 * hh + 2048)
        osl = slice(512 * c, 512 * c + 512)
        qw_core = np.empty((4, 128, 8 * 128), dtype=np.int32)
        for ot in range(4):
            for p in range(8):
                qw_core[ot, :, 128 * p:128 * p + 128] = \
                    qweight[p, 512 * c + 128 * ot:512 * c + 128 * ot + 128, :]
        maps.append({
            "xT": np.ascontiguousarray(x[b].T),
            "x_rows": np.ascontiguousarray(x[b, 1024 * hh:1024 * hh + 1024, :]),
            "wqT": np.ascontiguousarray(Wq[dsl, :].T),
            "wkT": np.ascontiguousarray(Wk[dsl, :].T),
            "wvT": np.ascontiguousarray(Wv[dsl, :].T),
            "woT": np.ascontiguousarray(Wo.T[dsl, :]),
            "bq_t": np.ascontiguousarray(bq[dsl].reshape(16, 128).T),
            "bk_t": np.ascontiguousarray(bk[dsl].reshape(16, 128).T),
            "bv_r": np.ascontiguousarray(bv[dsl].reshape(1, 2048)),
            "bo_r": np.ascontiguousarray(bo.reshape(1, 4096)),
            "fb_r": np.ascontiguousarray(fbias.reshape(1, 4096)),
            "qw32": qw_core,
            "lut4_t": np.ascontiguousarray(
                lut4[osl].reshape(4, 128, 16).transpose(1, 0, 2).reshape(128, 64)),
            "lut8_t": np.ascontiguousarray(
                lut8[osl].reshape(4, 128, 256).transpose(1, 0, 2).reshape(128, 1024)),
            "ident": ident,
            "ones_r": ones,
            "trineg": tri,
        })
    return maps


def kernel(**inputs):
    maps = _layout_inputs(inputs)
    nc = build_program()
    res = run_bass_kernel_spmd(nc, maps, core_ids=list(range(8)))
    out = np.empty((B, L, D), dtype=np.float32)
    for c in range(8):
        b, hh = c // 2, c % 2
        out[b, 1024 * hh:1024 * hh + 1024, :] = res.results[c]["out_rows"]
    return out
